# version 10
# baseline (speedup 1.0000x reference)
import numpy as np
import concourse.bass as bass
import concourse.mybir as mybir
from concourse.bass_utils import run_bass_kernel_spmd

F32 = mybir.dt.float32
BF16 = mybir.dt.bfloat16
I32 = mybir.dt.int32
A = mybir.AluOpType
AF = mybir.ActivationFunctionType
AX = mybir.AxisListType.X

NCORES = 8
EPS_LN = 1e-5
EPS_DEN = 1e-6
QMAX = 32767.0
WQMAX = 7.0
INV_U16 = float(np.float32(2.0 ** -16))
MAGIC = 12582912.0

FULL = dict(T=2048, D=1024, H=16, I=3072)


class Prog:

    ENGS = ("sy", "sc", "ve", "gp", "pe")
    R = 6

    def __init__(self):
        self.ops = {e: [] for e in self.ENGS}
        self.semnames = list(self.ENGS) + [
            f"{e}d{r}" for e in ("sy", "sc", "gp") for r in range(self.R)]
        self.count = {s: 0 for s in self.semnames}
        self.dma_i = {e: 0 for e in ("sy", "sc", "gp")}

    def op(self, eng, fn, deps=(), inc=0, tag=""):
        deps = [d for d in deps if d is not None]
        for d in deps:
            assert isinstance(d, tuple) and d[0] in self.semnames, (tag, d)
        if inc:
            self.count[eng] += inc
            self.ops[eng].append((fn, list(deps), eng, self.count[eng], tag))
            return (eng, self.count[eng])
        self.ops[eng].append((fn, list(deps), None, 0, tag))
        return None

    def dma(self, eng, out, in_, deps=(), tag="", **kw):
        deps = [d for d in deps if d is not None]
        r = self.dma_i[eng] % self.R
        self.dma_i[eng] += 1
        sem = f"{eng}d{r}"
        prev = self.count[sem]
        waits = list(deps) + ([(sem, prev)] if prev else [])
        self.count[sem] = prev + 16
        self.ops[eng].append((
            lambda e, out=out, in_=in_, kw=kw: e.dma_start(out=out, in_=in_, **kw),
            waits, sem, self.count[sem], tag))
        return (sem, self.count[sem])

    def verify(self):
        idx = {e: 0 for e in self.ENGS}
        done = {s: 0 for s in self.semnames}
        progressed = True
        while progressed:
            progressed = False
            for e in self.ENGS:
                while idx[e] < len(self.ops[e]):
                    fn, waits, sem, val, tag = self.ops[e][idx[e]]
                    if all(done[ws] >= wv for (ws, wv) in waits):
                        if sem is not None:
                            done[sem] = max(done[sem], val)
                        idx[e] += 1
                        progressed = True
                    else:
                        break
        stuck = {e: idx[e] for e in self.ENGS if idx[e] < len(self.ops[e])}
        if stuck:
            lines = []
            for e, i in stuck.items():
                fn, waits, sem, val, tag = self.ops[e][i]
                bad = [(ws, wv, done[ws]) for (ws, wv) in waits if done[ws] < wv]
                lines.append(f"{e}[{i}] tag={tag} blocked on {bad}")
            raise RuntimeError("schedule deadlock:\n" + "\n".join(lines))

    def emit_engine(self, e, eng, sems):
        hw = {s: 0 for s in self.semnames}
        for fn, waits, sem, val, tag in self.ops[e]:
            need = {}
            for (ws, wv) in waits:
                need[ws] = max(need.get(ws, 0), wv)
            for ws, wv in need.items():
                if wv > hw[ws]:
                    eng.wait_ge(sems[ws], wv)
                    hw[ws] = wv
            ins = fn(eng)
            if sem is not None:
                assert ins is not None, tag
                ins.then_inc(sems[sem], val - (self.ops[e][0] and 0) if False
                             else (16 if sem not in self.ENGS else 1))


class Arena:
    def __init__(self, sbuf_tensor, n_f32):
        self.t = sbuf_tensor
        self.n = n_f32
        self.off = 0

    def f32(self, n):
        ap = self.t.ap()[:, self.off:self.off + n]
        self.off += n
        assert self.off <= self.n, f"arena overflow {self.off}>{self.n}"
        return ap

    def bf16(self, n):
        n = n + (n % 2)
        ap = self.t.ap()[:, self.off:self.off + n // 2].bitcast(BF16)[:, 0:n]
        self.off += n // 2
        assert self.off <= self.n, f"arena overflow {self.off}>{self.n}"
        return ap

    def reset(self, off=0):
        self.off = off


def build_nc(cfg, stage=99, p2depth=9, dbg=False):
    T, D, H, I = cfg["T"], cfg["D"], cfg["H"], cfg["I"]
    DH = D // H
    assert DH == 64
    C = 128
    NCH = T // C
    KD = D // 128
    KI = I // 128
    TUG = max(1, (KI + NCORES - 1) // NCORES)
    TD = max(1, (KD + NCORES - 1) // NCORES)
    RUG = TUG * 128
    RD = TD * 128
    NT = max(1, T // 512)
    W5 = min(512, T)
    VW = DH + 1
    HP = H // 2
    PB = min(4, HP)
    NKB = (HP + PB - 1) // PB
    HB = min(4, H // 2)
    DW = min(512, D)
    ND = D // DW
    NBK = (D + 511) // 512
    PW = min(512, D)
    assert NBK * 3 <= 6 and NKB <= 2

    nc = bass.Bass("TRN2", target_bir_lowering=False, debug=False)

    x_in = nc.dram_tensor("x", [T, D], F32, kind="ExternalInput")
    W_in = {w: nc.dram_tensor(w, [D, D], F32, kind="ExternalInput")
            for w in ("Wq", "Wk", "Wv", "Wo")}
    wup_in = nc.dram_tensor("wup_sl", [RUG, D], F32, kind="ExternalInput")
    wg_in = nc.dram_tensor("wg_sl", [RUG, D], F32, kind="ExternalInput")
    wd_in = nc.dram_tensor("wd_sl", [RD, I], F32, kind="ExternalInput")
    identb_in = nc.dram_tensor("identb", [128, 128], BF16, kind="ExternalInput")
    identf_in = nc.dram_tensor("identf", [128, 128], F32, kind="ExternalInput")
    mask_in = nc.dram_tensor("mask", [128, HB * 128], F32, kind="ExternalInput")
    ctab_in = nc.dram_tensor("ctab", [C, NCH * 32], F32, kind="ExternalInput")
    stab_in = nc.dram_tensor("stab", [C, NCH * 32], F32, kind="ExternalInput")
    flags_in = nc.dram_tensor("flags", [1, 8], F32, kind="ExternalInput")
    out_hbm = nc.dram_tensor("out", [T, D], F32, kind="ExternalOutput")
    if dbg:
        dbgb = nc.dram_tensor("dbgb", [8, 128, T], BF16, kind="ExternalOutput")
        dbgf = nc.dram_tensor("dbgf", [4, 128, T], F32, kind="ExternalOutput")
        dbgw = nc.dram_tensor("dbgw", [2, 128, KI * 128], BF16, kind="ExternalOutput")
        dbgd = nc.dram_tensor("dbgd", [2, 128, 512], F32, kind="ExternalOutput")

    d_qT = nc.dram_tensor("d_qT", [NCH, 128, KD * 128], BF16)
    d_kT = nc.dram_tensor("d_kT", [NCH, 128, KD * 128], BF16)
    d_va = nc.dram_tensor("d_va", [NCH, 128, H * VW], BF16)
    d_ck = nc.dram_tensor("d_ck", [NCH, 128, HP * VW], BF16)
    d_up = nc.dram_tensor("d_up", [KI, 128, T], BF16)
    d_gt = nc.dram_tensor("d_gt", [KI, 128, T], BF16)
    d_P = nc.dram_tensor("d_P", [KI, 128, T], BF16)
    d_og = nc.dram_tensor("d_og", [KI, 128, T], BF16)
    d_oq = nc.dram_tensor("d_oq", [KI, 128, T], BF16)
    d_woT = nc.dram_tensor("d_woT", [128, KD * D], BF16)
    d_attn = nc.dram_tensor("d_attn", [T, D], F32)
    d_agu_i = nc.dram_tensor("d_agu_i", [KD * 128, RUG], BF16)
    d_agu_o = nc.dram_tensor("d_agu_o", [NCORES, KD, 128, RUG], BF16)
    d_agg_i = nc.dram_tensor("d_agg_i", [KD * 128, RUG], BF16)
    d_agg_o = nc.dram_tensor("d_agg_o", [NCORES, KD, 128, RUG], BF16)
    d_agd_i = nc.dram_tensor("d_agd_i", [TD, 128, I], BF16)
    d_agd_o = nc.dram_tensor("d_agd_o", [NCORES, TD, 128, I], BF16)
    d_ar = {r: (nc.dram_tensor(f"d_ar{r}_i", [1, 2], F32),
                nc.dram_tensor(f"d_ar{r}_o", [1, 2], F32)) for r in "ABCD"}
    d_kvx_i = nc.dram_tensor("d_kvx_i", [128, HP * VW], F32)
    d_kvx_o = nc.dram_tensor("d_kvx_o", [128, HP * VW], F32)
    d_red = nc.dram_tensor("d_red", [2, 128], F32)

    from contextlib import ExitStack
    es = ExitStack()

    def sbuf(name, shape, dt=F32):
        return es.enter_context(nc.sbuf_tensor(name, shape, dt))

    nA = max((KD * T + 1) // 2, 4 * T)
    arA = Arena(sbuf("aA", [128, nA], F32), nA)
    xnT = arA.bf16(KD * T)
    arA.reset(0)
    stq = [[arA.f32(T), arA.f32(T)] for _ in range(2)]

    nBv = (max(3 * KD * D, KI * D) + 1) // 2
    arB = Arena(sbuf("aB", [128, nBv], F32), nBv)
    WT = {w: arB.bf16(KD * D) for w in ("Wq", "Wk", "Wv")}
    arB.reset(0)
    wdT = arB.bf16(KI * D)

    WQW = max(D, I)
    nCv = (KD * D + 1) // 2 + max(
        2 * WQW + (KD * RUG + 1) // 2 + (KD * 128 + 1) // 2,
        (KD * T + 1) // 2) + 8
    arC = Arena(sbuf("aC", [128, nCv], F32), nCv)
    s_wno = arC.bf16(KD * D)
    _mkC = arC.off
    wq_ld = arC.f32(WQW)
    wq_u = arC.f32(WQW)
    s_shT = arC.bf16(KD * RUG)
    s_woS = arC.bf16(KD * 128)
    arC.reset(_mkC)
    x_qT = arC.bf16(KD * T)

    def seg1_alloc(ar):
        o = {}
        o["q_f"] = [ar.f32(D) for _ in range(2)]
        o["k_f"] = [ar.f32(D) for _ in range(2)]
        o["rq_f"] = [ar.f32(D) for _ in range(2)]
        o["rk_f"] = [ar.f32(D) for _ in range(2)]
        o["ctab"] = ar.f32(NCH * 32)
        o["stab"] = ar.f32(NCH * 32)
        o["s_xnb"] = ar.bf16(D)
        o["s_qb"] = ar.bf16(D)
        o["s_kb"] = ar.bf16(D)
        o["s_tq"] = ar.bf16(KD * 128)
        o["s_tk"] = ar.bf16(KD * 128)
        o["s_va"] = [ar.bf16(H * VW) for _ in range(2)]
        o["ck_bf"] = ar.bf16(HP * VW)
        return o

    def seg2_alloc(ar):
        o = {}
        o["s_q2"] = [ar.bf16(KD * 128) for _ in range(2)]
        o["s_k2"] = [ar.bf16(KD * 128) for _ in range(2)]
        o["s_v2"] = [ar.bf16(H * VW) for _ in range(2)]
        o["s_c2"] = [ar.bf16(HP * VW) for _ in range(2)]
        o["s_am"] = ar.bf16((H // 2) * 128)
        o["s_y"] = ar.bf16(D)
        o["s_yT"] = ar.bf16(KD * 128)
        o["GwB"] = ar.bf16(2 * T)
        if isinstance(o["GwB"], tuple):
            o["Gw"] = [o["GwB"]] * 2
        else:
            o["Gw"] = [o["GwB"][:, i * T:(i + 1) * T] for i in range(2)]
        o["s_upr"] = [ar.bf16(T) for _ in range(2)]
        o["s_wdn"] = ar.bf16(I)
        o["s_oq"] = [ar.bf16(KI * 128)]
        return o

    class _Count:
        def __init__(self):
            self.off = 0

        def f32(self, n):
            o = self.off
            self.off += n
            return ("f32", o, n)

        def bf16(self, n):
            n = n + (n % 2)
            o = self.off
            self.off += n // 2
            return ("bf16", o, n)

    c1, c2 = _Count(), _Count()
    seg1_alloc(c1)
    seg2_alloc(c2)
    nDv = max(c1.off, c2.off) + 4
    arD = Arena(sbuf("aD", [128, nDv], F32), nDv)
    g1 = seg1_alloc(arD)
    arD.reset(0)
    g2 = seg2_alloc(arD)
    q_f, k_f, rq_f, rk_f = g1["q_f"], g1["k_f"], g1["rq_f"], g1["rk_f"]
    ctab, stab = g1["ctab"], g1["stab"]
    s_xnb, s_qb, s_kb = g1["s_xnb"], g1["s_qb"], g1["s_kb"]
    s_tq, s_tk, s_va = g1["s_tq"], g1["s_tk"], g1["s_va"]
    ck_bf = g1["ck_bf"]
    s_q2, s_k2, s_v2, s_c2 = g2["s_q2"], g2["s_k2"], g2["s_v2"], g2["s_c2"]
    s_am, s_y, s_yT = g2["s_am"], g2["s_y"], g2["s_yT"]
    Gw, s_upr, s_wdn, s_oq = g2["Gw"], g2["s_upr"], g2["s_wdn"], g2["s_oq"]
    s_oq2 = g2["GwB"][:, 0:KI * 128]
    s_ws_t = sbuf("s_wsded", [128, 2 * KD * 128], BF16)
    s_ws = [s_ws_t.ap()[:, i * KD * 128:(i + 1) * KD * 128] for i in range(2)]
    s_ev_t = sbuf("s_evded", [128, 4 * 512], BF16)
    s_evB = s_ev_t.ap()
    s_ev = [s_evB[:, i * 512:(i + 1) * 512] for i in range(4)]

    a_E = sbuf("aE", [128, 2 * D], F32)
    xst = [a_E.ap()[:, i * D:(i + 1) * D] for i in range(2)]

    identb = sbuf("sb_identb", [128, 128], BF16)
    identf = sbuf("sb_identf", [128, 128], F32)
    stG_bf = sbuf("stG_bf", [128, HP * VW], BF16)
    ones_row = sbuf("ones_row", [1, 128], F32)
    s_fr = sbuf("s_fr", [1, 8], F32)
    s_r2 = sbuf("s_r2", [2, 1], F32)
    maskT = sbuf("maskT", [128, HB * 128], F32)
    s_yf = sbuf("s_yf", [128, (H // 2) * DH], F32)
    state = sbuf("state", [128, HP * VW], F32)
    st_bf = state[:].bitcast(BF16)[:, 0:HP * VW]
    stG = sbuf("stG", [128, HP * VW], F32)
    s_rc = sbuf("s_rc", [128, H], F32)
    pUG = sbuf("pUG", [128, max(2 * KI, NCH)], F32)
    pmax = pUG[:, 0:NCH]
    pPO = sbuf("pPO", [128, 2 * KI], F32)
    red1 = sbuf("red1", [1, 4], F32)
    redp = sbuf("redp", [128, 4], F32)
    rowB = sbuf("rowred", [1, 128], F32)
    row = rowB[0:1, 0:64]
    row2 = rowB[0:1, 64:128]
    gmx = sbuf("gmx", [128, 8], F32)
    scl = sbuf("scl", [128, 18], F32)
    flg = sbuf("flg", [128, 2], F32)
    NBmax = max(D // 32, I // 32)
    wbm = sbuf("wbm", [128, NBmax], F32)
    whs = sbuf("whs", [128, NBmax], F32)
    nst = 6 * ((D + 511) // 512)
    s_bn = sbuf("s_bn", [128, nst], F32)
    s_mv = sbuf("s_mv", [128, 2], F32)

    (S_SX, S_INVX, S_SU, S_INVU, S_SG, S_INVG,
     S_S3, S_C3, S_SO, S_C5, S_T0, S_T1, S_T2, S_NCX, S_BSG,
     S_MG, S_NMG) = range(17)

    ps = [es.enter_context(nc.psum_tensor(f"ps{i}", [128, 512], F32)) for i in range(6)]
    pt = [es.enter_context(nc.psum_tensor(f"pt{i}", [128, 1024], BF16)) for i in range(2)]

    P = Prog()
    sems = {s: es.enter_context(nc.semaphore(f"sem_{s}")) for s in P.semnames}

    def TT(eng, out, i0, i1, op, deps, tag, inc=1):
        return P.op(eng, lambda e: e.tensor_tensor(out, i0, i1, op), deps, inc, tag)

    def TS(eng, out, i0, s1, s2, o0, o1, deps, tag, inc=1):
        if s2 is None:
            return P.op(eng, lambda e: e.tensor_single_scalar(out, i0, s1, o0),
                        deps, inc, tag)
        return P.op(eng, lambda e: e.tensor_scalar(out, i0, s1, s2, o0, o1),
                    deps, inc, tag)

    def STT(eng, out, i0, s, i1, o0, o1, deps, tag, inc=1):
        return P.op(eng, lambda e: e.scalar_tensor_tensor(out, i0, s, i1, o0, o1),
                    deps, inc, tag)

    def RED(eng, out, in_, op, deps, tag, absv=False, inc=1):
        return P.op(eng, lambda e: e.tensor_reduce(out, in_, AX, op,
                                                   apply_absolute_value=absv),
                    deps, inc, tag)

    def CP(eng, out, in_, deps, tag, inc=1):
        if eng == "sc":
            return P.op(eng, lambda e: e.copy(out, in_), deps, inc, tag)
        return P.op(eng, lambda e: e.tensor_copy(out, in_), deps, inc, tag)

    def RECIP(out, in_, deps, tag):
        return P.op("ve", lambda e: e.reciprocal(out, in_), deps, 1, tag)

    def ACT(out, in_, func, bias, scale, deps, tag, inc=1):
        return P.op("sc", lambda e: e.activation(out, in_, func, bias=bias,
                                                 scale=scale), deps, inc, tag)

    def MM(out, lhsT, rhs, start, stop, deps, tag, inc=0, tile=None):
        return P.op("pe", lambda e: e.matmul(out, lhsT, rhs, start=start,
                                             stop=stop, tile_position=tile),
                    deps, inc, tag)

    def TRN(out, in_, deps, tag, inc=0):
        return P.op("pe", lambda e: e.transpose(out, in_, identb[:]), deps, inc, tag)

    def CC(kind, op, groups, ins, outs, deps, tag):
        return P.op("gp", lambda e: e.collective_compute(kind, op,
                                                         replica_groups=groups,
                                                         ins=ins, outs=outs),
                    deps, 1, tag)

    def cover(toks):
        best = {}
        for t in toks:
            if t is None:
                continue
            s, v = t
            if best.get(s, 0) < v:
                best[s] = v
        return [(s, v) for s, v in best.items()]

    def xpart_max(cols_ap, ncols, out_row_ap, bank, deps, tag):
        t1 = P.op("pe", lambda e: e.transpose(bank[0:ncols, 0:128], cols_ap,
                                              identf[:]),
                  list(deps) + [tld_identf], 1, tag + ".tp")
        t2 = RED("ve", out_row_ap, bank[0:ncols, 0:128], A.max, [t1], tag + ".rd")
        return t2

    def bcast_mm(row_ap, n, dst_ap, bank, deps, tag):
        t1 = MM(bank[:, 0:n], ones_row[0:1, 0:128], row_ap, True, True,
                list(deps) + [t_ones], tag + ".mm", inc=1)
        t2 = CP("ve", dst_ap, bank[:, 0:n], [t1], tag + ".cp")
        return t2

    tp_state = dict(g=0, evtok=[])

    def transpose_group(srcs, dst, deps, tag, evac_deps=(), dst3=False):
        g = tp_state["g"]
        tp_state["g"] += 1
        bank = pt[g % 2]
        d0 = list(deps) + [tld_identb]
        if g >= 2:
            d0.append(tp_state["evtok"][g - 2])
        tpe = None
        for bi, s in enumerate(srcs):
            tpe = TRN(bank[:, bi * 128:(bi + 1) * 128], s,
                      d0 if bi == 0 else [], f"{tag}.t{bi}",
                      inc=(1 if bi == len(srcs) - 1 else 0))
        src_ap = bank[:, 0:len(srcs) * 128]
        if dst3:
            src_ap = src_ap.rearrange("p (k d) -> p k d", d=128)
        te = CP("sc", dst, src_ap, [tpe] + list(evac_deps),
                f"{tag}.ev")
        tp_state["evtok"].append(te)
        return te


    def scale_from_max(gcol, s_col, inv_col, qdiv, deps, tag):
        a = RECIP(scl[:, S_T2:S_T2 + 1], gcol, deps, tag + ".r1")
        b = TS("ve", scl[:, S_T2:S_T2 + 1], scl[:, S_T2:S_T2 + 1], qdiv, None,
               A.mult, None, [a], tag + ".m")
        c0 = RECIP(s_col, scl[:, S_T2:S_T2 + 1], [b], tag + ".r2")
        d = TS("ve", s_col, s_col, 1e-12, None, A.max, None, [c0], tag + ".mx")
        e0 = RECIP(inv_col, s_col, [d], tag + ".ri") if inv_col is not None else d
        return d, e0


    def emit_tail():
        deps = [(s, P.count[s]) for s in P.semnames if P.count[s] > 0]
        td = P.dma("sc", out_hbm.ap()[0:C, :], xst[0], deps=deps, tag="tail.out")
        P.op("gp", lambda e: e.nop(), [td], 1, "fin")
        P.verify()
        blk = es.enter_context(nc.Block())
        for _e, _attr in (("sy", "sync"), ("sc", "scalar"), ("ve", "vector"),
                          ("gp", "gpsimd"), ("pe", "tensor")):
            getattr(blk, _attr)(lambda eng, _e=_e: P.emit_engine(_e, eng, sems))
        es.close()
        return nc

    tld_identb = P.dma("sy", identb[:], identb_in.ap(), tag="ld.identb")
    tld_identf = P.dma("sy", identf[:], identf_in.ap(), tag="ld.identf")
    t_ones = P.op("ve", lambda e: e.memset(ones_row[0:1, :], 1.0), (), 1, "ones_row")
    tld_mask = P.dma("sy", maskT[:], mask_in.ap(), tag="ld.mask")
    tld_ctab = P.dma("sy", ctab, ctab_in.ap(), tag="ld.ctab")
    tld_stab = P.dma("sy", stab, stab_in.ap(), tag="ld.stab")
    tld_fr = P.dma("sy", s_fr[0:1, :], flags_in.ap(), tag="ld.fr")
    tld_flg = bcast_mm(s_fr[0:1, 0:2], 2, flg[:, 0:2], ps[0], [tld_fr], "flgb")
    t_state0 = P.op("ve", lambda e: e.memset(state[:], 0.0), (), 1, "state0")
    t_mg0 = P.op("ve", lambda e: e.memset(scl[:, S_MG:S_MG + 1], MAGIC),
                 (), 1, "mgc")
    t_mg = P.op("ve", lambda e: e.memset(scl[:, S_NMG:S_NMG + 1], -MAGIC),
                [t_mg0], 1, "nmgc")
    t_vaones = []
    for i in range(2):
        t_vaones.append(P.op(
            "ve",
            lambda e, i=i: e.memset(
                s_va[i].rearrange("p (h w) -> p h w", w=VW)[:, :, DH:DH + 1], 1.0),
            (), 1, f"vaones{i}"))

    if stage <= 1:
        return emit_tail()
    t_wT = {}
    prev_tok = None
    woS_guard = None
    for w in ("Wq", "Wk", "Wv", "Wo"):
        t_cast = P.dma(
            "gp", s_wno.rearrange("p (k m) -> p k m", k=KD),
            W_in[w].ap().rearrange("(k p) m -> p k m", p=128),
            deps=([] if prev_tok is None else [prev_tok]), tag=f"cast.{w}")
        last = None
        for ic in range(KD):
            srcs = [s_wno[:, oc * D + ic * 128:oc * D + ic * 128 + 128]
                    for oc in range(KD)]
            if w != "Wo":
                last = transpose_group(srcs, WT[w][:, ic * D:ic * D + KD * 128],
                                       [t_cast], f"wT.{w}.{ic}")
            else:
                ev = transpose_group(srcs, s_woS, [t_cast], f"wT.Wo.{ic}",
                                     evac_deps=([woS_guard] if woS_guard else []))
                last = P.dma("sy", d_woT.ap()[:, ic * KD * 128:(ic + 1) * KD * 128],
                             s_woS, deps=[ev], tag=f"woT.sp{ic}")
                woS_guard = last
        t_wT[w] = last
        prev_tok = last

    t_agu_in_holder = [None]

    if stage <= 2:
        return emit_tail()
    nwt = 2 * TUG + TD
    t_shard = {}
    t_agd_in = []
    prev_use = None
    for t in range(nwt):
        wid = D if t < 2 * TUG else I
        nb = wid // 32
        if t < TUG:
            src_h = wup_in.ap()[(t % TUG) * 128:(t % TUG) * 128 + 128, :]
        elif t < 2 * TUG:
            u = t - TUG
            src_h = wg_in.ap()[u * 128:u * 128 + 128, :]
        else:
            u = t - 2 * TUG
            src_h = wd_in.ap()[u * 128:u * 128 + 128, :]
        tl = P.dma("sy", wq_ld[:, 0:wid], src_h,
                   deps=([prev_use] if prev_use else []), tag=f"wq.ld{t}")
        src3 = wq_ld[:, 0:wid].rearrange("p (b j) -> p b j", j=32)
        t_bm = RED("ve", wbm[:, 0:nb], src3, A.max, [tl], f"wq.bm{t}", absv=True)
        ta = RECIP(whs[:, 0:nb], wbm[:, 0:nb], [t_bm], f"wq.sa{t}")
        tb = TS("ve", whs[:, 0:nb], whs[:, 0:nb], WQMAX, None, A.mult, None,
                [ta], f"wq.sb{t}")
        tc = RECIP(wbm[:, 0:nb], whs[:, 0:nb], [tb], f"wq.sc{t}")
        t_s = TS("ve", wbm[:, 0:nb], wbm[:, 0:nb], 1e-12, None, A.max, None,
                 [tc], f"wq.s{t}")
        t_h = RECIP(whs[:, 0:nb], wbm[:, 0:nb], [t_s], f"wq.h{t}")
        bss = wbm[:, 0:nb].rearrange("p (b o) -> p b o", o=1).broadcast_to([128, nb, 32])
        bhs = whs[:, 0:nb].rearrange("p (b o) -> p b o", o=1).broadcast_to([128, nb, 32])
        u3 = wq_u[:, 0:wid].rearrange("p (b j) -> p b j", j=32)
        t_u = TT("gp", u3, src3, bhs, A.mult, [t_h], f"wq.u{t}")
        t_u2 = TS("ve", wq_u[:, 0:wid], wq_u[:, 0:wid], MAGIC, None, A.add,
                  None, [t_u], f"wq.u2{t}")
        t_u3 = TS("ve", wq_u[:, 0:wid], wq_u[:, 0:wid], -MAGIC, None, A.add,
                  None, [t_u2], f"wq.u3{t}")
        dst_bf = wq_ld[:, 0:wid].bitcast(BF16)[:, 0:wid]
        t_q = TT("gp", dst_bf.rearrange("p (b j) -> p b j", j=32), u3, bss,
                 A.mult, [t_u3], f"wq.q{t}")
        if t < 2 * TUG:
            u = t % TUG
            last = None
            for ic in range(KD):
                src = dst_bf[:, ic * 128:(ic + 1) * 128]
                dstp = s_shT[:, ic * RUG + u * 128:ic * RUG + u * 128 + 128]
                extra = []
                if t == TUG:
                    extra = [t_agu_in_holder[0]]
                last = transpose_group([src], dstp, [t_q] + extra, f"shT.{t}.{ic}")
            t_shard[t] = last
            prev_use = last
        else:
            u = t - 2 * TUG
            td = P.dma("gp", d_agd_i.ap()[u], dst_bf, deps=[t_q], tag=f"agd.i{t}")
            t_agd_in.append(td)
            prev_use = td
        if t == TUG - 1:
            t_agu_in_holder[0] = P.dma(
                "gp", d_agu_i.ap().rearrange("(k p) m -> p k m", p=128),
                s_shT.rearrange("p (k m) -> p k m", k=KD),
                deps=[t_shard[tt2] for tt2 in range(TUG)], tag="agu.i")
            prev_use = prev_use
    t_agu_in = t_agu_in_holder[0]
    t_ag_up = CC("AllGather", A.bypass, [list(range(NCORES))],
                 [d_agu_i.ap().opt()], [d_agu_o.ap().opt()], [t_agu_in], "ag.up")
    t_agg_in = P.dma("gp", d_agg_i.ap().rearrange("(k p) m -> p k m", p=128),
                     s_shT.rearrange("p (k m) -> p k m", k=KD),
                     deps=[t_shard[TUG + tt2] for tt2 in range(TUG)], tag="agg.i")
    t_ag_g = CC("AllGather", A.bypass, [list(range(NCORES))],
                [d_agg_i.ap().opt()], [d_agg_o.ap().opt()], [t_agg_in], "ag.gt")
    t_ag_d = CC("AllGather", A.bypass, [list(range(NCORES))],
                [d_agd_i.ap().opt()], [d_agd_o.ap().opt()], t_agd_in, "ag.dn")

    if stage <= 3:
        return emit_tail()
    t_ln = []
    t_xnT = []
    for c in range(NCH):
        deps = [t_ln[c - 2]] if c >= 2 else []
        tl = P.dma("sy", xst[c % 2], x_in.ap()[c * C:(c + 1) * C, :], deps=deps,
                   tag=f"ln.x{c}")
        nsl = (D + 511) // 512
        t1 = None
        for sl in range(nsl):
            w0 = min(512, D - sl * 512)
            t1 = P.op("ve", lambda e, c=c, sl=sl, w0=w0: e.bn_stats(
                s_bn[:, sl * 6:(sl + 1) * 6], xst[c % 2][:, sl * 512:sl * 512 + w0]),
                [tl] if sl == 0 else [], 1, f"ln.bs{c}.{sl}")
        t2 = P.op("ve", lambda e: e.bn_aggr(s_mv[:], s_bn[:]), [t1], 1, f"ln.ba{c}")
        t3 = TS("ve", s_mv[:, 1:2], s_mv[:, 1:2], EPS_LN, None, A.add, None,
                [t2], f"ln.ve{c}")
        t4 = ACT(s_mv[:, 1:2], s_mv[:, 1:2], AF.Sqrt, 0.0, 1.0, [t3], f"ln.sq{c}")
        t5 = RECIP(s_mv[:, 1:2], s_mv[:, 1:2], [t4], f"ln.rc{c}")
        t6 = TS("ve", rq_f[c % 2], xst[c % 2], s_mv[:, 0:1], s_mv[:, 1:2],
                A.subtract, A.mult,
                [t5] + ([t_xnT[c - 2]] if c >= 2 else []), f"ln.ap{c}")
        t_ln.append(t6)
        t7 = RED("ve", pmax[:, c:c + 1], rq_f[c % 2], A.max, [t6], f"ln.mx{c}",
                 absv=True)
        t8 = CP("ve", s_xnb, rq_f[c % 2], [t7], f"ln.cb{c}")
        srcs = [s_xnb[:, ic * 128:(ic + 1) * 128] for ic in range(KD)]
        dst = xnT.rearrange("p (k t) -> p k t", k=KD)[:, :, c * 128:(c + 1) * 128]
        t_xnT.append(transpose_group(srcs, dst, [t8], f"xnT.{c}"))
    t_cmb = RED("ve", redp[:, 0:1], pmax[:, 0:NCH], A.max, [t_ln[NCH - 1]], "xa.r1")
    t_gx = xpart_max(redp[:, 0:1], 1, red1[0:1, 0:1], ps[0],
                     [t_cmb, tld_flg], "xa")
    t_arA_w = P.dma("gp", d_ar["A"][0].ap()[0:1, 0:1], red1[0:1, 0:1],
                    deps=[t_gx], tag="arA.w")
    t_arA = CC("AllReduce", A.max, [list(range(NCORES))],
               [d_ar["A"][0].ap()[0:1, 0:1]], [d_ar["A"][1].ap()[0:1, 0:1]],
               [t_arA_w], "arA")
    t_arA_r = P.dma("gp", row[0:1, 0:1], d_ar["A"][1].ap()[0:1, 0:1],
                    deps=[t_arA], tag="arA.r")

    assert 2 * D >= T
    wide2 = a_E.ap()[:, 0:T]
    t_gmx0 = bcast_mm(row[0:1, 0:1], 1, gmx[:, 0:1], ps[0], [t_arA_r, t_gx],
                      "arA.b")
    t_sx, t_invx = scale_from_max(
        gmx[:, 0:1], scl[:, S_SX:S_SX + 1], scl[:, S_INVX:S_INVX + 1],
        QMAX, [t_gmx0], "sx")
    t_ncx = TS("ve", scl[:, S_NCX:S_NCX + 1], scl[:, S_SX:S_SX + 1], -MAGIC,
               None, A.mult, None, [t_sx], "ncx")
    xq_dead = [t_agu_in, t_agg_in, t_wT["Wo"], t_mg] + cover(t_agd_in)
    t_xq = None
    for ic in range(KD):
        sl = x_qT[:, ic * T:(ic + 1) * T]
        d0 = [t_invx, t_ncx, t_mg, t_ln[-1], t_ln[-2]] + cover(t_xnT)
        if ic == 0:
            d0 += xq_dead
        if t_xq is not None:
            d0.append(t_xq)
        t1 = ACT(wide2, xnT[:, ic * T:(ic + 1) * T], AF.Identity,
                 scl[:, S_MG:S_MG + 1], scl[:, S_INVX:S_INVX + 1], d0,
                 f"xq.y{ic}")
        t_xq = ACT(sl, wide2, AF.Identity, scl[:, S_NCX:S_NCX + 1],
                   scl[:, S_SX:S_SX + 1], [t1], f"xq.s{ic}")
    t_xq_done = t_xq

    if stage <= 4:
        return emit_tail()
    xnT3 = xnT.rearrange("p (k t) -> p k t", k=KD)
    t_evq, t_evk, t_evv = [], [], []
    t_ropeq, t_ropek = [], []
    t_cast_q, t_cast_k = [], []
    t_spq, t_spk, t_spv, t_ck = [], [], [], []
    t_stup = [t_state0]
    qkv_toks = []
    mm_prev2 = [None, None]
    ev_prev = [None]
    spill_toks = []
    absmax_toks = []

    def qkv_mms(c):
        deps0 = [t_xnT[c], t_wT["Wq"], t_wT["Wk"], t_wT["Wv"], t_stup[-1]]
        if c == 0:
            deps0.append(t_gmx0)
        if c >= 1:
            deps0 += [t_evq[c - 1], t_evk[c - 1], t_evv[c - 1]]
        if ev_prev[0] is not None:
            deps0.append(ev_prev[0])
        out = []
        for ic in range(KD):
            lhs = xnT3[:, ic, c * 128:(c + 1) * 128]
            for pi, w in enumerate(("Wq", "Wk", "Wv")):
                for hh in range(NBK):
                    bank = ps[pi * NBK + hh]
                    rhs = WT[w][:, ic * D + hh * PW:ic * D + hh * PW + PW]
                    tk3 = MM(bank[:, 0:PW], lhs, rhs, ic == 0, ic == KD - 1,
                             deps0 if (ic == 0 and pi == 0 and hh == 0) else [],
                             f"qkv{c}.{ic}.{pi}.{hh}",
                             inc=(1 if ic == KD - 1 else 0))
                    if ic == KD - 1 and hh == NBK - 1:
                        out.append(tk3)
        qkv_toks.append(tuple(out))

    cw = 32

    def rope_phi(src_f, dst_f, scr_f, dep_ev, pre, tag):
        c = int(tag.split("_")[-1])
        cs = ctab[:, c * cw:(c + 1) * cw].rearrange("p (o j) -> p o j", o=1).broadcast_to([128, H, cw])
        sn = stab[:, c * cw:(c + 1) * cw].rearrange("p (o j) -> p o j", o=1).broadcast_to([128, H, cw])
        s3 = src_f.rearrange("p (h d) -> p h d", d=DH)
        o3 = dst_f.rearrange("p (h d) -> p h d", d=DH)
        sc3 = scr_f.rearrange("p (h d) -> p h d", d=DH)
        q1, q2 = s3[:, :, 0:cw], s3[:, :, cw:2 * cw]
        a1 = TT("gp", o3[:, :, 0:cw], q1, cs, A.mult,
                [dep_ev, tld_ctab, tld_stab] + pre, tag + ".a1")
        a2 = TT("gp", sc3[:, :, 0:cw], q2, sn, A.mult, [dep_ev] + pre, tag + ".a2")
        a3 = TT("gp", o3[:, :, 0:cw], o3[:, :, 0:cw], sc3[:, :, 0:cw],
                A.subtract, [a1, a2], tag + ".a3")
        b1 = TT("gp", o3[:, :, cw:2 * cw], q1, sn, A.mult, [dep_ev], tag + ".b1")
        b2 = TT("gp", sc3[:, :, cw:2 * cw], q2, cs, A.mult, [a2], tag + ".b2")
        b3 = TT("gp", o3[:, :, cw:2 * cw], o3[:, :, cw:2 * cw], sc3[:, :, cw:2 * cw],
                A.add, [b1, b2], tag + ".b3")
        p1 = TS("ve", scr_f, dst_f, 0.0, None, A.min, None, [a3, b3], tag + ".p1")
        p2 = ACT(scr_f, scr_f, AF.Exp, 0.0, 1.0, [p1], tag + ".p2")
        p3 = TS("ve", dst_f, dst_f, 0.0, None, A.max, None, [p1], tag + ".p3")
        p4 = TT("gp", dst_f, dst_f, scr_f, A.add, [p2, p3], tag + ".p4")
        return p4

    def pass1_rest(c):
        tq, tk_, tv = qkv_toks[c]
        dq = [tq] + ([t_ropek[c - 2]] if c >= 2 else [])
        dk = [tk_] + ([t_ropek[c - 2]] if c >= 2 else [])
        dv = [tv, t_vaones[c % 2]]
        if c >= 2:
            dv += [t_spv[c - 2]]
        if c >= 1:
            dv.append(t_stup[-1])
        par = c % 2
        eq = ek = ev = None
        for hh in range(NBK):
            eq = CP("sc", q_f[par][:, hh * PW:hh * PW + PW], ps[0 * NBK + hh][:, 0:PW],
                    dq if hh == 0 else [], f"evq{c}.{hh}")
        for hh in range(NBK):
            ek = CP("sc", k_f[par][:, hh * PW:hh * PW + PW], ps[1 * NBK + hh][:, 0:PW],
                    dk if hh == 0 else [], f"evk{c}.{hh}")
        va3 = s_va[c % 2].rearrange("p (h w) -> p h w", w=VW)
        nhh = PW // DH
        for hh in range(NBK):
            ev = CP("sc", va3[:, hh * nhh:(hh + 1) * nhh, 0:DH],
                    ps[2 * NBK + hh][:, 0:PW].rearrange("p (h w) -> p h w", w=DH),
                    dv if hh == 0 else [], f"evv{c}.{hh}")
        t_evq.append(eq)
        t_evk.append(ek)
        t_evv.append(ev)
        rq = rope_phi(q_f[par], rq_f[par], rk_f[par], eq,
                      [t_ropek[c - 2]] if c >= 2 else [], f"rq_{c}")
        t_ropeq.append(rq)
        rk = rope_phi(k_f[par], rk_f[par], q_f[par], ek, [rq], f"rk_{c}")
        t_ropek.append(rk)

    t_gq, t_gk = [], []

    def pass1_tail(c):
        cq = CP("ve", s_qb, rq_f[c % 2],
                [t_ropeq[c]] + ([t_gq[c - 1]] if c >= 1 else []), f"castq{c}")
        ckk = CP("ve", s_kb, rk_f[c % 2],
                 [t_ropek[c]] + ([t_gk[c - 1], t_stup[-1]] if c >= 1 else []),
                 f"castk{c}")
        t_cast_q.append(cq)
        t_cast_k.append(ckk)
        gq = transpose_group([s_qb[:, ic * 128:(ic + 1) * 128] for ic in range(KD)],
                             s_tq, [cq],
                             f"tq{c}", evac_deps=([t_spq[c - 1]] if c >= 1 else []))
        gk = transpose_group([s_kb[:, ic * 128:(ic + 1) * 128] for ic in range(KD)],
                             s_tk, [ckk],
                             f"tk{c}", evac_deps=([t_spk[c - 1]] if c >= 1 else []))
        t_gq.append(gq)
        t_gk.append(gk)
        t_spq.append(P.dma("sy", d_qT.ap()[c], s_tq, deps=[gq], tag=f"spq{c}"))
        t_spk.append(P.dma("sy", d_kT.ap()[c], s_tk, deps=[gk], tag=f"spk{c}"))
        t_spv.append(P.dma("sy", d_va.ap()[c], s_va[c % 2], deps=[t_evv[c]],
                           tag=f"spv{c}"))
        bank_free = t_evv[c + 1] if c + 1 < len(t_evv) else t_evv[c]
        kvdep = [t_cast_k[c], t_evv[c], bank_free, t_stup[-1]]
        kv_toks = []
        for h in range(H):
            j = h // 2
            bank = ps[2 * NBK + (0 if j < PB else 1)]
            colbase = (j % PB) * VW
            outap = bank[(h % 2) * 64:(h % 2) * 64 + 64, colbase:colbase + VW]
            rhs = s_va[c % 2][:, h * VW:(h + 1) * VW]
            tkv = MM(outap, s_kb[:, h * DH:(h + 1) * DH], rhs,
                     True, True, kvdep if h == 0 else [],
                     f"kv{c}.{h}", inc=(1 if h % 2 == 1 else 0),
                     tile=((0, 64) if h % 2 == 1 else None))
            if h % 2 == 1:
                kv_toks.append(tkv)
        tck = CP("ve", ck_bf, state[:],
                 [t_stup[-1]] + ([t_ck[c - 1]] if c >= 1 else []), f"ck{c}")
        t_ck.append(P.dma("sy", d_ck.ap()[c], ck_bf, deps=[tck], tag=f"ckw{c}"))
        upds = None
        for b in range(NKB):
            lo = b * PB * VW
            hi = min(HP, (b + 1) * PB) * VW
            upds = TT("ve", state[:, lo:hi], ps[2 * NBK + b][:, 0:hi - lo],
                      state[:, lo:hi], A.add,
                      [kv_toks[min(len(kv_toks) - 1, (b + 1) * PB - 1)], tck],
                      f"stup{c}.{b}")
        t_stup.append(upds)

    spill_toks = []
    absmax_toks = []
    ffn_parts = [(0, d_up, d_agu_o, t_ag_up, "up"),
                 (1, d_gt, d_agg_o, t_ag_g, "gt")]

    def ffn_block(idx):
        u, dsp, ag_o, t_ag, pref = ffn_parts[idx // KI]
        Ic = idx % KI
        r, uu = divmod(Ic, TUG)
        r = r % NCORES
        wdeps = [t_ag]
        if idx >= 2:
            wdeps.append(mm_prev2[idx % 2])
        twl = P.dma("sy", s_ws[idx % 2].rearrange("p (k m) -> p k m", k=KD),
                    ag_o.ap()[r, :, :, uu * 128:(uu + 1) * 128].rearrange(
                        "k p m -> p k m"),
                    deps=wdeps, tag=f"{pref}.w{Ic}")
        mm_last = None
        for ic in range(KD):
            lhs = s_ws[idx % 2][:, ic * 128:(ic + 1) * 128]
            for tt_ in range(NT):
                bank = ps[tt_ % 4]
                d0 = []
                if ic == 0 and tt_ == 0:
                    d0 = [twl, t_xq_done]
                    if ev_prev[0] is not None:
                        d0.append(ev_prev[0])
                    if t_evq:
                        d0 += [t_evq[-1], t_evk[-1]]
                rhs = x_qT[:, ic * T + tt_ * W5:ic * T + tt_ * W5 + W5]
                mm_last = MM(bank[:, 0:W5], lhs, rhs, ic == 0, ic == KD - 1,
                             d0, f"{pref}.mm{Ic}.{ic}.{tt_}",
                             inc=(1 if ic == KD - 1 else 0))
        mm_prev2[idx % 2] = mm_last
        ev_toks_ic = []
        g0 = len(spill_toks) % 4
        for tt_ in range(NT):
            g = len(spill_toks)
            evd = [mm_last]
            if g >= 4:
                evd += [spill_toks[g - 4], absmax_toks[g - 4]]
            tev = CP("sc", s_ev[g % 4][:, 0:W5], ps[tt_ % 4][:, 0:W5], evd,
                     f"{pref}.ev{Ic}.{tt_}")
            ev_toks_ic.append(tev)
            tsp = P.dma("sy", dsp.ap()[Ic, :, tt_ * W5:tt_ * W5 + W5],
                        s_ev[g % 4][:, 0:W5], deps=[tev],
                        tag=f"{pref}.sp{Ic}.{tt_}")
            spill_toks.append(tsp)
            ev_prev[0] = tev
        tmx = RED("ve", pUG[:, u * KI + Ic:u * KI + Ic + 1],
                  s_evB[:, g0 * W5:(g0 + NT) * W5], A.max, ev_toks_ic,
                  f"{pref}.mx{Ic}", absv=True)
        absmax_toks.extend([tmx] * NT)

    NBLK = 2 * KI
    nffn = 0
    qkv_mms(0)
    pass1_rest(0)
    for c in range(NCH):
        if c + 1 < NCH:
            qkv_mms(c + 1)
            pass1_rest(c + 1)
        pass1_tail(c)
        tgt = 0 if c < 2 else ((c - 1) * NBLK) // (NCH - 2)
        while nffn < tgt:
            ffn_block(nffn)
            nffn += 1
    while nffn < NBLK:
        ffn_block(nffn)
        nffn += 1

    if stage <= 5:
        return emit_tail()
    t_ctb = TS("ve", stG[:], state[:], flg[:, 0:1], None, A.mult, None,
               [t_stup[-1], tld_flg], "kvx.ctb")
    t_kvw = P.dma("gp", d_kvx_i.ap(), stG[:], deps=[t_ctb], tag="kvx.w")
    t_kvx = CC("AllReduce", A.add,
               [[2 * i, 2 * i + 1] for i in range(NCORES // 2)],
               [d_kvx_i.ap()], [d_kvx_o.ap()], [t_kvw], "kvx")
    t_kvr = P.dma("gp", stG[:], d_kvx_o.ap(), deps=[t_kvx], tag="kvx.r")
    t_gmask0 = TS("ve", stG[:], stG[:], flg[:, 1:2], None, A.mult, None,
                  [t_kvr], "kvx.m")
    t_gmask = CP("ve", stG_bf[:], stG[:], [t_gmask0], "kvx.cb")

    last_qkv = t_evv[NCH - 1]
    pass1_dead = [t_cast_q[-1], t_cast_k[-1], t_stup[-1], t_ctb,
                  t_gmask, tp_state["evtok"][-1]] + cover(
                      t_ck + t_spv + t_spq + t_spk +
                      list(qkv_toks[-1]))

    if stage <= 6:
        return emit_tail()
    t_rB1 = RED("ve", redp[:, 1:2], pUG[:, 0:KI], A.max, absmax_toks, "rB.u")
    t_rB2 = RED("ve", redp[:, 2:3], pUG[:, KI:2 * KI], A.max,
                absmax_toks, "rB.g")
    t_gB = xpart_max(redp[:, 1:3], 2, s_r2[0:2, 0:1], ps[0],
                     [t_rB1, t_rB2, t_gmx0], "rB")
    t_arBw = P.dma("gp", d_ar["B"][0].ap().rearrange("a p -> p a"),
                   s_r2.ap(), deps=[t_gB], tag="arB.w")
    t_arB = CC("AllReduce", A.max, [list(range(NCORES))],
               [d_ar["B"][0].ap()], [d_ar["B"][1].ap()], [t_arBw], "arB")
    t_arB_r = P.dma("gp", row[0:1, 0:2], d_ar["B"][1].ap()[0:1, 0:2],
                    deps=[t_arB], tag="arB.r")
    t_gmxB = bcast_mm(row[0:1, 0:2], 2, gmx[:, 1:3], ps[0],
                      [t_arB_r, t_gB], "arB.b")
    t_su, t_invu = scale_from_max(
        gmx[:, 1:2], scl[:, S_SU:S_SU + 1], scl[:, S_INVU:S_INVU + 1],
        QMAX, [t_gmxB], "su")
    t_sg, t_invg = scale_from_max(
        gmx[:, 2:3], scl[:, S_SG:S_SG + 1], scl[:, S_INVG:S_INVG + 1],
        QMAX, [t_gmxB], "sg")
    t_bsg = TS("ve", scl[:, S_BSG:S_BSG + 1], scl[:, S_SG:S_SG + 1], -MAGIC,
               None, A.mult, None, [t_sg], "bsg")

    if stage <= 7:
        return emit_tail()
    t_woT_ld = P.dma("sy", s_wno, d_woT.ap(), deps=[t_wT["Wo"], t_agg_in],
                     tag="woT.ld")

    t_wdT_groups = []

    def emit_wdT(tile_idx):
        r, uu = divmod(tile_idx, TD)
        r = r % NCORES
        deps = [t_ag_d, last_qkv] + (pass1_dead if tile_idx == 0 else [])
        if tile_idx >= 1:
            deps.append(t_wdT_groups[-1])
        tl = P.dma("sy", s_wdn, d_agd_o.ap()[r, uu], deps=deps,
                   tag=f"wdT.ld{tile_idx}")
        last = None
        o = tile_idx * 128
        wdT3 = wdT.rearrange("p (k d) -> p k d", d=D)
        for g0 in range(0, KI, 8):
            ng = min(8, KI - g0)
            srcs = [s_wdn[:, (g0 + i) * 128:(g0 + i + 1) * 128]
                    for i in range(ng)]
            dst = wdT3[:, g0:g0 + ng, o:o + 128]
            last = transpose_group(srcs, dst, [tl], f"wdT.{tile_idx}.{g0}",
                                   dst3=True)
        t_wdT_groups.append(last)

    t_p2_ld = []
    t_y = []
    t_out = []
    prev_am = [t_gmxB]
    prev_yf = [None]
    prev_numev = [None]
    prev_num_mm = [None]
    prev_wo = [None]

    def emit_pass2_chunk(c):
        rot = c % 2
        extra = pass1_dead if c < 2 else [t_p2_ld[c - 2][0]]
        tq2 = P.dma("sy", s_q2[rot], d_qT.ap()[c], deps=[t_spq[c]] + extra,
                    tag=f"p2q{c}")
        tk2 = P.dma("sy", s_k2[rot], d_kT.ap()[c],
                    deps=[t_spk[c]] + (pass1_dead if c < 2 else [t_p2_ld[c - 2][1]]),
                    tag=f"p2k{c}")
        tv2 = P.dma("sy", s_v2[rot], d_va.ap()[c],
                    deps=[t_spv[c]] + (pass1_dead if c < 2 else [t_p2_ld[c - 2][2]]),
                    tag=f"p2v{c}")
        tc2 = P.dma("sy", s_c2[rot], d_ck.ap()[c],
                    deps=[t_ck[c]] + (pass1_dead if c < 2 else [t_p2_ld[c - 2][3]]),
                    tag=f"p2c{c}")
        t_p2_ld.append((tq2, tk2, tv2, tc2))
        if p2depth <= 1:
            return
        ts2 = TT("ve", st_bf, s_c2[rot], stG_bf[:], A.add,
                 [tc2, t_gmask] + ([prev_num_mm[0]] if prev_num_mm[0] else []),
                 f"p2st{c}")
        if p2depth <= 2:
            return
        ydeps = []
        for hh in range(2):
            h0 = hh * (H // 2)
            amdeps = [prev_am[0]] if prev_am[0] else []
            am_toks = []
            NHH = H // 2
            for h in range(h0, h0 + NHH):
                b = (h - h0) % 2
                bank = ps[b]
                col = ((h - h0) // 2) * 128
                bp = (h % 2) * 64
                lhs = s_k2[rot][bp:bp + 64, (h // 2) * 128:(h // 2) * 128 + 128]
                rhs = s_q2[rot][bp:bp + 64, (h // 2) * 128:(h // 2) * 128 + 128]
                tk3 = MM(bank[:, col:col + 128], lhs, rhs, True, True,
                         ([tq2, tk2] + amdeps) if h == h0 else [],
                         f"ab{c}.{h}", inc=1,
                         tile=((bp, 0) if bp else None))
                am_toks.append(tk3)
            ev_am = []
            for b in range(2):
                n_in = NHH // 2
                mb = maskT.ap()[:, 0:n_in * 128].rearrange("p (h t) -> p h t", t=128)
                dep_ev = [am_toks[-1], am_toks[-2], tld_mask]
                if prev_num_mm[0]:
                    dep_ev.append(prev_num_mm[0])
                s_am_v = s_am.rearrange("p (i two t) -> p two i t", two=2, t=128)[:, b]
                tev = TT("ve", s_am_v,
                         ps[b][:, 0:n_in * 128].rearrange("p (h t) -> p h t", t=128),
                         mb, A.mult, dep_ev, f"amev{c}.{hh}.{b}")
                ev_am.append(tev)
            prev_am[0] = ev_am[-1]
            if p2depth <= 3:
                continue
            num_toks = []
            for h in range(h0, h0 + NHH):
                b = (h - h0) % 2
                bank = ps[2 + b]
                col = ((h - h0) // 2) * VW
                bp = (h % 2) * 64
                lhsq = s_q2[rot][bp:bp + 64, (h // 2) * 128:(h // 2) * 128 + 128]
                rhs_st = st_bf[bp:bp + 64, (h // 2) * VW:(h // 2) * VW + VW]
                d0 = [ts2, ev_am[-1]]
                if prev_numev[0]:
                    d0.append(prev_numev[0])
                MM(bank[:, col:col + VW], lhsq, rhs_st, True, False,
                   d0 if h == h0 else [], f"nump{c}.{h}",
                   tile=((bp, 0) if bp else None))
                lhsa = s_am[:, (h - h0) * 128:(h - h0 + 1) * 128]
                rhsv = s_v2[rot][:, h * VW:(h + 1) * VW]
                t2b = MM(bank[:, col:col + VW], lhsa, rhsv, False, True,
                         [tv2] if h == h0 else [], f"numi{c}.{h}", inc=1)
                num_toks.append(t2b)
            prev_num_mm[0] = num_toks[-1]
            for b in range(2):
                n_in = NHH // 2
                bank = ps[2 + b]
                b3 = bank[:, 0:n_in * VW].rearrange("p (h w) -> p h w", w=VW)
                rc_v = s_rc.ap().rearrange("p (hh i two) -> p hh two i", hh=2,
                                           two=2)[:, hh, b]
                td1 = TS("ve", rc_v,
                         b3[:, :, DH:DH + 1].rearrange("p h x -> p (h x)"),
                         EPS_DEN, None, A.max, None,
                         [num_toks[-1], num_toks[-2]], f"den{c}.{hh}.{b}")
                td2 = RECIP(rc_v, rc_v, [td1], f"rc{c}.{hh}.{b}")
                rcb = rc_v.rearrange("p (h o) -> p h o", o=1).broadcast_to([128, n_in, DH])
                yf3 = s_yf.ap()[:, 0:n_in * DH].rearrange("p (h d) -> p h d", d=DH)
                td3 = TT("ve", yf3, b3[:, :, 0:DH], rcb, A.mult,
                         [td2] + ([prev_yf[0]] if prev_yf[0] else []),
                         f"ydiv{c}.{hh}.{b}")
                sy_v = s_y.rearrange("p (hh i two d) -> p hh two i d", hh=2,
                                     two=2, d=DH)[:, hh, b]
                td4 = CP("ve", sy_v, s_yf[:, 0:n_in * DH], [td3],
                         f"ycast{c}.{hh}.{b}")
                prev_yf[0] = td4
                prev_numev[0] = td4
                ydeps.append(td4)
        if p2depth <= 4:
            return
        gy = transpose_group([s_y[:, ic * 128:(ic + 1) * 128] for ic in range(KD)],
                             s_yT, ydeps, f"yT{c}",
                             evac_deps=([t_y[c - 1]] if c >= 1 else []))
        wo_toks = []
        for ic in range(KD):
            lhs = s_yT[:, ic * 128:(ic + 1) * 128]
            for dd in range(ND):
                d0 = []
                if ic == 0 and dd == 0:
                    d0 = [gy, t_woT_ld]
                    if prev_wo[0]:
                        d0.append(prev_wo[0])
                rhs = s_wno[:, ic * D + dd * DW:ic * D + dd * DW + DW]
                tw = MM(ps[4 + dd][:, 0:DW], lhs, rhs, ic == 0, ic == KD - 1,
                        d0, f"wo{c}.{ic}.{dd}", inc=(1 if ic == KD - 1 else 0))
                if ic == KD - 1:
                    wo_toks.append(tw)
        t_y.append(gy)
        if p2depth <= 5:
            t_y.append(gy)
            return
        txl = P.dma("sc", xst[c % 2], x_in.ap()[c * C:(c + 1) * C, :],
                    deps=([t_out[c - 2]] if c >= 2
                          else [t_ln[-1], t_xnT[-1], t_xq_done]),
                    tag=f"oxl{c}")
        ev_toks = []
        for dd in range(ND):
            tev = TT("ve", xst[c % 2][:, dd * DW:(dd + 1) * DW],
                     xst[c % 2][:, dd * DW:(dd + 1) * DW], ps[4 + dd][:, 0:DW],
                     A.add, [txl, wo_toks[dd]], f"oadd{c}.{dd}")
            ev_toks.append(tev)
        prev_wo[0] = ev_toks[-1]
        t_out.append(P.dma("sc", d_attn.ap()[c * C:(c + 1) * C, :], xst[c % 2],
                           deps=ev_toks, tag=f"out{c}"))

    gate_P_toks = []
    t_gmxs = []

    def emit_gate_chunk(j):
        rot = j % 2
        deps = [spill_toks[(KI + j) * NT + tt] for tt in range(NT)]
        deps += (pass1_dead if j < 2 else [gate_P_toks[j - 2], t_gmxs[j - 2]])
        tl = P.dma("sy", Gw[rot], d_gt.ap()[j], deps=deps, tag=f"g.ld{j}")
        sA, sB = stq[rot]
        d0 = [tl, t_invg, t_bsg, t_mg]
        if j >= 2:
            d0.append(gate_P_toks[j - 2])
        t2 = ACT(sA, Gw[rot], AF.Identity, scl[:, S_MG:S_MG + 1],
                 scl[:, S_INVG:S_INVG + 1], d0, f"g.z{j}")
        t3 = ACT(sB, sA, AF.Sigmoid, scl[:, S_BSG:S_BSG + 1],
                 scl[:, S_SG:S_SG + 1], [t2], f"g.sig{j}")
        t5 = ACT(sB, sB, AF.Identity, scl[:, S_MG:S_MG + 1], 65536.0,
                 [t3], f"g.z2{j}")
        t6 = TS("ve", sB, sB, INV_U16, -192.0, A.mult, A.add, [t5], f"g.zb{j}")
        t6b = TS("ve", sB, sB, 0.9999847412109375, None, A.min, None, [t6],
                 f"g.zc{j}")
        t7 = STT("ve", Gw[rot], sA, -MAGIC, sB, A.add, A.mult, [t6b], f"g.P{j}")
        t8 = RED("ve", pPO[:, j:j + 1], Gw[rot], A.max, [t7], f"g.mx{j}", absv=True)
        t_gmxs.append(t8)
        gate_P_toks.append(P.dma("sy", d_P.ap()[j], Gw[rot], deps=[t7],
                                 tag=f"g.sp{j}"))

    def emit_round_C():
        t1 = RED("ve", redp[:, 3:4], pPO[:, 0:KI], A.max, t_gmxs, "rC.c")
        t4 = xpart_max(redp[:, 3:4], 1, red1[0:1, 3:4], ps[0],
                       [t1, prev_am[0]], "rC")
        t5 = P.dma("gp", d_ar["C"][0].ap()[0:1, 0:1], red1[0:1, 3:4], deps=[t4],
                   tag="arC.w")
        t6 = CC("AllReduce", A.max, [list(range(NCORES))],
                [d_ar["C"][0].ap()[0:1, 0:1]], [d_ar["C"][1].ap()[0:1, 0:1]],
                [t5], "arC")
        t6r = P.dma("gp", row2[0:1, 0:1], d_ar["C"][1].ap()[0:1, 0:1],
                    deps=[t6], tag="arC.r")
        t7 = bcast_mm(row2[0:1, 0:1], 1, gmx[:, 3:4], ps[0], [t6r, t4], "arC.b")
        prev_am[0] = t7
        u1 = TS("ve", scl[:, S_T0:S_T0 + 1], gmx[:, 3:4], scl[:, S_SG:S_SG + 1],
                None, A.mult, None, [t7, t_sg, t_invg], "s3.a")
        u2, _ = scale_from_max(scl[:, S_T0:S_T0 + 1], scl[:, S_S3:S_S3 + 1],
                               None, QMAX, [u1], "s3.b")
        u3 = RECIP(scl[:, S_C3:S_C3 + 1], scl[:, S_S3:S_S3 + 1], [u2], "s3.r")
        u4 = TS("ve", scl[:, S_C3:S_C3 + 1], scl[:, S_C3:S_C3 + 1],
                scl[:, S_SG:S_SG + 1], None, A.mult, None, [u3], "s3.c")
        return u4, u2

    o_toks = []
    t_omxs = []

    def emit_o_chunk(j, tc3, tn3):
        rot = j % 2
        deps = [gate_P_toks[j]]
        if j >= 2:
            deps += [o_toks[j - 2], t_omxs[j - 2]]
        else:
            deps += cover(gate_P_toks) + [t_gmxs[-1], t_gmxs[-2]]
        tl = P.dma("sy", Gw[rot], d_P.ap()[j], deps=deps, tag=f"o.Pld{j}")
        tu = P.dma("sy", s_upr[rot], d_up.ap()[j],
                   deps=[spill_toks[j * NT + tt] for tt in range(NT)] +
                   ([o_toks[j - 2]] if j >= 2 else pass1_dead), tag=f"o.uld{j}")
        sA, sB = stq[rot]
        d0 = [tl, tc3, tn3, t_mg]
        if j >= 2:
            d0.append(o_toks[j - 2])
        t2 = ACT(sA, Gw[rot], AF.Identity, scl[:, S_MG:S_MG + 1],
                 scl[:, S_C3:S_C3 + 1], d0, f"o.z3{j}")
        t4 = ACT(sB, s_upr[rot], AF.Identity, scl[:, S_MG:S_MG + 1],
                 scl[:, S_INVU:S_INVU + 1], [tu, t_invu, t_mg], f"o.z4{j}")
        t4b = TS("ve", sB, sB, -MAGIC, None, A.add, None, [t4], f"o.z4b{j}")
        t5 = STT("ve", Gw[rot], sA, -MAGIC, sB, A.add, A.mult, [t2, t4b],
                 f"o.P2{j}")
        t6 = RED("ve", pPO[:, KI + j:KI + j + 1], Gw[rot], A.max, [t5],
                 f"o.mx{j}", absv=True)
        t_omxs.append(t6)
        o_toks.append(P.dma("sy", d_og.ap()[j], Gw[rot], deps=[t5],
                            tag=f"o.sp{j}"))

    def emit_round_D():
        t1 = RED("ve", redp[:, 0:1], pPO[:, KI:2 * KI], A.max, t_omxs, "rD.c")
        t4 = xpart_max(redp[:, 0:1], 1, red1[0:1, 0:1], ps[0],
                       [t1, prev_am[0]], "rD")
        t5 = P.dma("gp", d_ar["D"][0].ap()[0:1, 0:1], red1[0:1, 0:1], deps=[t4],
                   tag="arD.w")
        t6 = CC("AllReduce", A.max, [list(range(NCORES))],
                [d_ar["D"][0].ap()[0:1, 0:1]], [d_ar["D"][1].ap()[0:1, 0:1]],
                [t5], "arD")
        t6r = P.dma("gp", row2[0:1, 0:1], d_ar["D"][1].ap()[0:1, 0:1],
                    deps=[t6], tag="arD.r")
        t7 = bcast_mm(row2[0:1, 0:1], 1, gmx[:, 4:5], ps[0], [t6r, t4], "arD.b")
        prev_am[0] = t7
        u1 = TS("ve", scl[:, S_T1:S_T1 + 1], scl[:, S_S3:S_S3 + 1],
                scl[:, S_SU:S_SU + 1], None, A.mult, None, [t7], "so.a")
        u2 = TS("ve", scl[:, S_T0:S_T0 + 1], gmx[:, 4:5], scl[:, S_T1:S_T1 + 1],
                None, A.mult, None, [u1], "so.b")
        u3, _ = scale_from_max(scl[:, S_T0:S_T0 + 1], scl[:, S_SO:S_SO + 1],
                               None, QMAX, [u2], "so.c")
        u4 = RECIP(scl[:, S_C5:S_C5 + 1], scl[:, S_SO:S_SO + 1], [u3], "so.r")
        u5 = TS("ve", scl[:, S_C5:S_C5 + 1], scl[:, S_C5:S_C5 + 1],
                scl[:, S_T1:S_T1 + 1], None, A.mult, None, [u4], "so.d")
        return u5, u3

    oq_toks = []

    def emit_oq_chunk(j, tc5, tn5):
        rot = j % 2
        deps = [o_toks[j]]
        if j >= 2:
            deps.append(oq_toks[j - 2])
        else:
            deps += cover(o_toks) + [t_omxs[-1], t_omxs[-2]]
        tl = P.dma("sy", Gw[rot], d_og.ap()[j], deps=deps, tag=f"q.ld{j}")
        sA = stq[rot][0]
        t1 = ACT(sA, Gw[rot], AF.Identity, scl[:, S_MG:S_MG + 1],
                 scl[:, S_C5:S_C5 + 1],
                 [tl, tc5, tn5, t_mg] + ([oq_toks[j - 2]] if j >= 2 else []),
                 f"q.y{j}")
        t2a = TS("ve", sA, sA, -MAGIC, None, A.add, None, [t1], f"q.um{j}")
        t2b = TS("ve", sA, sA, 32767.0, None, A.min, None, [t2a], f"q.cl{j}")
        t3 = ACT(Gw[rot], sA, AF.Identity, 0.0, scl[:, S_SO:S_SO + 1],
                 [t2b], f"q.oq{j}")
        oq_toks.append(P.dma("sy", d_oq.ap()[j], Gw[rot], deps=[t3],
                             tag=f"q.sp{j}"))

    t_dnld, t_dnev, t_dnw, t_dnmm = [], [], [], []
    dbg_hold = []

    def emit_down():
        first_extra = [prev_am[0], prev_numev[0], prev_wo[0]]
        for c in range(NCH):
            buf = s_oq[0] if c % 2 == 0 else s_oq2
            deps = cover(oq_toks) + [t_wdT_groups[-1]] + (first_extra if c < 2 else [])
            if c >= 2:
                deps.append(t_dnmm[c - 2])
            tl = P.dma("sy",
                       buf.rearrange("p (k t) -> p k t", t=128),
                       d_oq.ap()[:, :, c * 128:(c + 1) * 128].rearrange(
                           "k p t -> p k t"), deps=deps,
                       tag=f"dn.ld{c}")
            t_dnld.append(tl)
            mm_toks = []
            for dd in range(ND):
                bank = ps[(c % 2) * 2 + dd]
                last = None
                for ici in range(KI):
                    lhs = buf[:, ici * 128:(ici + 1) * 128]
                    d0 = []
                    if ici == 0 and dd == 0:
                        d0 = [tl] + (first_extra if c < 2 else [])
                        if c >= 2:
                            d0 += [t_dnev[(c - 2) * ND + d2] for d2 in range(ND)]
                    rhs = wdT[:, ici * D + dd * DW:ici * D + dd * DW + DW]
                    last = MM(bank[:, 0:DW], lhs, rhs, ici == 0, ici == KI - 1,
                              d0, f"dn.mm{c}.{dd}.{ici}",
                              inc=(1 if ici == KI - 1 else 0))
                mm_toks.append(last)
            t_dnmm.append(mm_toks[-1])
            if dbg and c == 2:
                tdb1 = CP("ve", s_yf[:, 0:512], ps[(c % 2) * 2 + (ND - 1)][:, 0:DW],
                          [mm_toks[ND - 1]], "dbg.psum")
                dbg_hold.append(P.dma("gp", dbgd.ap()[0, :, 0:DW], s_yf[:, 0:512],
                                      deps=[tdb1], tag="dbg.psw"))
                dbg_hold.append(tdb1)
            if dbg and c == 2 and len(dbg_hold) >= 0:
                pass
            tpl = P.dma("sc", stq[c % 2][0][:, 0:D], d_attn.ap()[c * C:(c + 1) * C, :],
                        deps=[t_out[c]] + (cover(oq_toks) if c < 2 else [t_dnw[(c - 2) * ND]]),
                        tag=f"dn.pl{c}")
            for dd in range(ND):
                dstf = stq[c % 2][1][:, dd * DW:(dd + 1) * DW]
                d0 = [mm_toks[dd], tpl] + cover(oq_toks)
                if c >= 2:
                    d0.append(t_dnw[(c - 2) * ND + dd])
                te1 = CP("sc", dstf, ps[(c % 2) * 2 + dd][:, 0:DW], d0,
                         f"dn.cp{c}.{dd}")
                tev = TT("ve", dstf, dstf,
                         stq[c % 2][0][:, dd * DW:(dd + 1) * DW], A.add,
                         [te1, tpl], f"dn.ev{c}.{dd}")
                t_dnev.append(tev)
                t_dnw.append(P.dma(
                    "gp", out_hbm.ap()[c * C:(c + 1) * C, dd * DW:(dd + 1) * DW],
                    dstf, deps=[tev], tag=f"dn.w{c}.{dd}"))

    gi = 0
    wi = 0
    nwd = KD
    do_g = stage not in (72, 73, 74)
    do_w = stage not in (71, 73, 74)
    do_p = stage not in (71, 72)
    np2 = 2 if stage == 73 else NCH
    for c in range(NCH):
        upto = ((c + 1) * KI) // NCH
        while gi < upto:
            if do_g:
                emit_gate_chunk(gi)
            gi += 1
        upto_w = ((c + 1) * nwd) // NCH
        while wi < upto_w:
            if do_w:
                emit_wdT(wi)
            wi += 1
        if do_p and c < np2:
            emit_pass2_chunk(c)
    while gi < KI:
        if do_g:
            emit_gate_chunk(gi)
        gi += 1
    while wi < nwd:
        if do_w:
            emit_wdT(wi)
        wi += 1
    if stage in (71, 72, 73, 74):
        return emit_tail()
    if stage <= 8:
        return emit_tail()
    tc3, tn3 = emit_round_C()
    for j in range(KI):
        emit_o_chunk(j, tc3, tn3)
    if stage <= 9:
        return emit_tail()
    tc5, tn5 = emit_round_D()
    for j in range(KI):
        emit_oq_chunk(j, tc5, tn5)
    emit_down()

    if dbg:
        dd = (cover(t_out) + cover(t_dnw) + cover(oq_toks) + cover(o_toks) +
              cover(gate_P_toks) + cover(spill_toks) + [prev_am[0], prev_numev[0]])
        P.dma("gp", dbgb.ap()[0], d_up.ap()[0], deps=dd, tag="dbg.up")
        P.dma("gp", dbgb.ap()[1], d_gt.ap()[0], deps=dd, tag="dbg.gt")
        P.dma("gp", dbgb.ap()[2], d_P.ap()[0], deps=dd, tag="dbg.P")
        P.dma("gp", dbgb.ap()[3], d_og.ap()[0], deps=dd, tag="dbg.og")
        P.dma("gp", dbgb.ap()[4], d_oq.ap()[0], deps=dd, tag="dbg.oq")
        P.dma("gp", dbgb.ap()[5, :, 0:KD * 128], d_qT.ap()[0], deps=dd, tag="dbg.qT")
        P.dma("gp", dbgb.ap()[6, :, 0:KD * 128], d_kT.ap()[0], deps=dd, tag="dbg.kT")
        P.dma("gp", dbgb.ap()[7, :, 0:H * VW], d_va.ap()[0], deps=dd, tag="dbg.va")
        tg1 = P.dma("gp", dbgf.ap()[0, :, 0:8], gmx[:, 0:8], deps=dd, tag="dbg.gmx")
        P.dma("gp", dbgf.ap()[0, :, 24:40], scl[:, 0:16], deps=dd, tag="dbg.scl")
        P.dma("gp", dbgf.ap()[1, :, 0:D], d_attn.ap()[2 * C:3 * C, :], deps=dd, tag="dbg.attn")
        P.dma("gp", dbgf.ap()[2, :, 0:HP * VW], stG[:], deps=dd, tag="dbg.stG")
        tg2 = P.dma("gp", dbgf.ap()[3, :, 0:HP * VW],
                    d_ck.ap()[NCH // 2], deps=dd, tag="dbg.ck")
        P.dma("gp", dbgw.ap()[0], s_oq[0], deps=dd + cover(t_dnld), tag="dbg.soq")
        P.dma("gp", dbgw.ap()[1], wdT[:, 0:KI * 128], deps=dd, tag="dbg.wdT")
        P.dma("gp", dbgd.ap()[1, :, 0:DW],
              a_A.ap()[:, T + DW:T + 2 * DW] if False else arA.t.ap()[:, T + DW:T + 2 * DW],
              deps=dd, tag="dbg.part")
        P.op("gp", lambda e: e.nop(),
             cover(t_out) + cover(t_dnw) + [tg1, tg2] +
             [(s, P.count[s]) for s in ("gpd0", "gpd1", "gpd2", "gpd3", "gpd4", "gpd5")],
             1, "fin")
    else:
        P.op("gp", lambda e: e.nop(), cover(t_out) + cover(t_dnw), 1, "fin")

    P.verify()
    blk = es.enter_context(nc.Block())

    @blk.sync
    def _(eng):
        P.emit_engine("sy", eng, sems)

    @blk.scalar
    def _(eng):
        P.emit_engine("sc", eng, sems)

    @blk.vector
    def _(eng):
        P.emit_engine("ve", eng, sems)

    @blk.gpsimd
    def _(eng):
        P.emit_engine("gp", eng, sems)

    @blk.tensor
    def _(eng):
        P.emit_engine("pe", eng, sems)

    es.close()
    return nc


def _host_tables(cfg, core):
    T, D, H = cfg["T"], cfg["D"], cfg["H"]
    DH = D // H
    NCH = T // 128
    half = core % 2
    pos0 = half * T
    import ml_dtypes
    identf = np.eye(128, dtype=np.float32)
    identb = identf.astype(ml_dtypes.bfloat16)
    HB_ = min(4, H // 2)
    mask1 = np.zeros((128, 128), dtype=np.float32)
    for p in range(128):
        mask1[p, p:] = 1.0
    mask = np.tile(mask1, (1, HB_))
    halfw = DH // 2
    inv_freq = (10000.0 ** (-np.arange(halfw, dtype=np.float64) / halfw))
    ctab = np.zeros((128, NCH * 32), dtype=np.float32)
    stab = np.zeros((128, NCH * 32), dtype=np.float32)
    for c in range(NCH):
        pos = pos0 + c * 128 + np.arange(128)
        ang = pos[:, None] * inv_freq[None, :]
        ctab[:, c * 32:(c + 1) * 32] = np.cos(ang).astype(np.float32)
        stab[:, c * 32:(c + 1) * 32] = np.sin(ang).astype(np.float32)
    flags = np.zeros((1, 8), dtype=np.float32)
    flags[0, 0] = 1.0 if half == 0 else 0.0
    flags[0, 1] = 1.0 if half == 1 else 0.0
    return identb, identf, mask, ctab, stab, flags


def _shard_inputs(cfg, inputs):
    T, D, H, I = cfg["T"], cfg["D"], cfg["H"], cfg["I"]
    KI = I // 128
    KD = D // 128
    TUG = max(1, (KI + NCORES - 1) // NCORES)
    TD = max(1, (KD + NCORES - 1) // NCORES)
    x = np.ascontiguousarray(np.asarray(inputs["L0_attn_x"], dtype=np.float32))
    B, Tfull, _ = x.shape
    maps = []
    for core in range(NCORES):
        b, half = core // 2, core % 2
        xs = np.ascontiguousarray(x[b, half * T:(half + 1) * T, :])
        identb, identf, mask, ctab, stab, flags = _host_tables(cfg, core)
        rows_ug = [((core * TUG + u) % KI) * 128 for u in range(TUG)]
        wup = np.concatenate([np.asarray(inputs["L0_ffn_w_up"], np.float32)[r:r + 128]
                              for r in rows_ug], axis=0)
        wg = np.concatenate([np.asarray(inputs["L0_ffn_w_gate"], np.float32)[r:r + 128]
                             for r in rows_ug], axis=0)
        rows_d = [((core * TD + u) % KD) * 128 for u in range(TD)]
        wd = np.concatenate([np.asarray(inputs["L0_ffn_w_down"], np.float32)[r:r + 128]
                             for r in rows_d], axis=0)
        maps.append({
            "x": xs,
            "Wq": np.ascontiguousarray(np.asarray(inputs["L0_attn_Wq"], np.float32)),
            "Wk": np.ascontiguousarray(np.asarray(inputs["L0_attn_Wk"], np.float32)),
            "Wv": np.ascontiguousarray(np.asarray(inputs["L0_attn_Wv"], np.float32)),
            "Wo": np.ascontiguousarray(np.asarray(inputs["L0_attn_Wo"], np.float32)),
            "wup_sl": np.ascontiguousarray(wup),
            "wg_sl": np.ascontiguousarray(wg),
            "wd_sl": np.ascontiguousarray(wd),
            "identb": identb,
            "identf": identf,
            "mask": mask,
            "ctab": ctab,
            "stab": stab,
            "flags": flags,
        })
    return maps


_NC_CACHE = {}


def kernel(L0_attn_x, L0_attn_Wq, L0_attn_Wk, L0_attn_Wv, L0_attn_Wo,
           L0_ffn_w_gate, L0_ffn_w_up, L0_ffn_w_down):
    cfg = FULL
    inputs = dict(L0_attn_x=L0_attn_x, L0_attn_Wq=L0_attn_Wq,
                  L0_attn_Wk=L0_attn_Wk, L0_attn_Wv=L0_attn_Wv,
                  L0_attn_Wo=L0_attn_Wo, L0_ffn_w_gate=L0_ffn_w_gate,
                  L0_ffn_w_up=L0_ffn_w_up, L0_ffn_w_down=L0_ffn_w_down)
    key = tuple(sorted(cfg.items()))
    if key not in _NC_CACHE:
        _NC_CACHE[key] = build_nc(cfg)
    nc = _NC_CACHE[key]
    in_maps = _shard_inputs(cfg, inputs)
    res = run_bass_kernel_spmd(nc, in_maps, core_ids=list(range(NCORES)))
    T, D = cfg["T"], cfg["D"]
    B = np.asarray(L0_attn_x).shape[0]
    out = np.zeros((B, 2 * T, D), dtype=np.float32)
    for core in range(NCORES):
        b, half = core // 2, core % 2
        out[b, half * T:(half + 1) * T, :] = res.results[core]["out"]
    return out


# version 11
# speedup vs baseline: 1.0825x; 1.0825x over previous
import numpy as np
import concourse.bass as bass
import concourse.mybir as mybir
from concourse.bass_utils import run_bass_kernel_spmd

F32 = mybir.dt.float32
BF16 = mybir.dt.bfloat16
I32 = mybir.dt.int32
A = mybir.AluOpType
AF = mybir.ActivationFunctionType
AX = mybir.AxisListType.X

NCORES = 8
EPS_LN = 1e-5
EPS_DEN = 1e-6
QMAX = 32767.0
WQMAX = 7.0
INV_U16 = float(np.float32(2.0 ** -16))
MAGIC = 12582912.0

FULL = dict(T=2048, D=1024, H=16, I=3072)


class Prog:

    ENGS = ("sy", "sc", "ve", "gp", "pe")
    R = 6

    def __init__(self):
        self.ops = {e: [] for e in self.ENGS}
        self.semnames = list(self.ENGS) + [
            f"{e}d{r}" for e in ("sy", "sc", "gp") for r in range(self.R)]
        self.count = {s: 0 for s in self.semnames}
        self.dma_i = {e: 0 for e in ("sy", "sc", "gp")}

    def op(self, eng, fn, deps=(), inc=0, tag=""):
        deps = [d for d in deps if d is not None]
        for d in deps:
            assert isinstance(d, tuple) and d[0] in self.semnames, (tag, d)
        if inc:
            self.count[eng] += inc
            self.ops[eng].append((fn, list(deps), eng, self.count[eng], tag))
            return (eng, self.count[eng])
        self.ops[eng].append((fn, list(deps), None, 0, tag))
        return None

    def dma(self, eng, out, in_, deps=(), tag="", **kw):
        deps = [d for d in deps if d is not None]
        r = self.dma_i[eng] % self.R
        self.dma_i[eng] += 1
        sem = f"{eng}d{r}"
        prev = self.count[sem]
        waits = list(deps) + ([(sem, prev)] if prev else [])
        self.count[sem] = prev + 16
        self.ops[eng].append((
            lambda e, out=out, in_=in_, kw=kw: e.dma_start(out=out, in_=in_, **kw),
            waits, sem, self.count[sem], tag))
        return (sem, self.count[sem])

    def verify(self):
        idx = {e: 0 for e in self.ENGS}
        done = {s: 0 for s in self.semnames}
        progressed = True
        while progressed:
            progressed = False
            for e in self.ENGS:
                while idx[e] < len(self.ops[e]):
                    fn, waits, sem, val, tag = self.ops[e][idx[e]]
                    if all(done[ws] >= wv for (ws, wv) in waits):
                        if sem is not None:
                            done[sem] = max(done[sem], val)
                        idx[e] += 1
                        progressed = True
                    else:
                        break
        stuck = {e: idx[e] for e in self.ENGS if idx[e] < len(self.ops[e])}
        if stuck:
            lines = []
            for e, i in stuck.items():
                fn, waits, sem, val, tag = self.ops[e][i]
                bad = [(ws, wv, done[ws]) for (ws, wv) in waits if done[ws] < wv]
                lines.append(f"{e}[{i}] tag={tag} blocked on {bad}")
            raise RuntimeError("schedule deadlock:\n" + "\n".join(lines))

    def emit_engine(self, e, eng, sems):
        hw = {s: 0 for s in self.semnames}
        for fn, waits, sem, val, tag in self.ops[e]:
            need = {}
            for (ws, wv) in waits:
                need[ws] = max(need.get(ws, 0), wv)
            for ws, wv in need.items():
                if wv > hw[ws]:
                    eng.wait_ge(sems[ws], wv)
                    hw[ws] = wv
            ins = fn(eng)
            if sem is not None:
                assert ins is not None, tag
                ins.then_inc(sems[sem], val - (self.ops[e][0] and 0) if False
                             else (16 if sem not in self.ENGS else 1))


class Arena:
    def __init__(self, sbuf_tensor, n_f32):
        self.t = sbuf_tensor
        self.n = n_f32
        self.off = 0

    def f32(self, n):
        ap = self.t.ap()[:, self.off:self.off + n]
        self.off += n
        assert self.off <= self.n, f"arena overflow {self.off}>{self.n}"
        return ap

    def bf16(self, n):
        n = n + (n % 2)
        ap = self.t.ap()[:, self.off:self.off + n // 2].bitcast(BF16)[:, 0:n]
        self.off += n // 2
        assert self.off <= self.n, f"arena overflow {self.off}>{self.n}"
        return ap

    def reset(self, off=0):
        self.off = off


def build_nc(cfg, stage=99, p2depth=9, dbg=False):
    T, D, H, I = cfg["T"], cfg["D"], cfg["H"], cfg["I"]
    DH = D // H
    assert DH == 64
    C = 128
    NCH = T // C
    KD = D // 128
    KI = I // 128
    TUG = max(1, (KI + NCORES - 1) // NCORES)
    TD = max(1, (KD + NCORES - 1) // NCORES)
    RUG = TUG * 128
    RD = TD * 128
    NT = max(1, T // 512)
    W5 = min(512, T)
    VW = DH + 1
    HP = H // 2
    PB = min(4, HP)
    NKB = (HP + PB - 1) // PB
    HB = min(4, H // 2)
    DW = min(512, D)
    ND = D // DW
    NBK = (D + 511) // 512
    PW = min(512, D)
    assert NBK * 3 <= 6 and NKB <= 2

    nc = bass.Bass("TRN2", target_bir_lowering=False, debug=False)

    x_in = nc.dram_tensor("x", [T, D], F32, kind="ExternalInput")
    W_in = {w: nc.dram_tensor(w, [D, D], F32, kind="ExternalInput")
            for w in ("Wq", "Wk", "Wv", "Wo")}
    wup_in = nc.dram_tensor("wup_sl", [RUG, D], F32, kind="ExternalInput")
    wg_in = nc.dram_tensor("wg_sl", [RUG, D], F32, kind="ExternalInput")
    wd_in = nc.dram_tensor("wd_sl", [RD, I], F32, kind="ExternalInput")
    identb_in = nc.dram_tensor("identb", [128, 128], BF16, kind="ExternalInput")
    identf_in = nc.dram_tensor("identf", [128, 128], F32, kind="ExternalInput")
    mask_in = nc.dram_tensor("mask", [128, HB * 128], F32, kind="ExternalInput")
    ctab_in = nc.dram_tensor("ctab", [C, NCH * 32], F32, kind="ExternalInput")
    stab_in = nc.dram_tensor("stab", [C, NCH * 32], F32, kind="ExternalInput")
    flags_in = nc.dram_tensor("flags", [1, 8], F32, kind="ExternalInput")
    out_hbm = nc.dram_tensor("out", [T, D], F32, kind="ExternalOutput")
    if dbg:
        dbgb = nc.dram_tensor("dbgb", [8, 128, T], BF16, kind="ExternalOutput")
        dbgf = nc.dram_tensor("dbgf", [4, 128, T], F32, kind="ExternalOutput")
        dbgw = nc.dram_tensor("dbgw", [2, 128, KI * 128], BF16, kind="ExternalOutput")
        dbgd = nc.dram_tensor("dbgd", [2, 128, 512], F32, kind="ExternalOutput")

    d_qT = nc.dram_tensor("d_qT", [NCH, 128, KD * 128], BF16)
    d_kT = nc.dram_tensor("d_kT", [NCH, 128, KD * 128], BF16)
    d_va = nc.dram_tensor("d_va", [NCH, 128, H * VW], BF16)
    d_ck = nc.dram_tensor("d_ck", [NCH, 128, HP * VW], BF16)
    d_up = nc.dram_tensor("d_up", [KI, 128, T], BF16)
    d_gt = nc.dram_tensor("d_gt", [KI, 128, T], BF16)
    d_P = nc.dram_tensor("d_P", [KI, 128, T], BF16)
    d_og = nc.dram_tensor("d_og", [KI, 128, T], BF16)
    d_oq = nc.dram_tensor("d_oq", [KI, 128, T], BF16)
    d_woT = nc.dram_tensor("d_woT", [128, KD * D], BF16)
    d_attn = nc.dram_tensor("d_attn", [T, D], F32)
    d_agu_i = nc.dram_tensor("d_agu_i", [KD * 128, RUG], BF16)
    d_agu_o = nc.dram_tensor("d_agu_o", [NCORES, KD, 128, RUG], BF16)
    d_agg_i = nc.dram_tensor("d_agg_i", [KD * 128, RUG], BF16)
    d_agg_o = nc.dram_tensor("d_agg_o", [NCORES, KD, 128, RUG], BF16)
    d_agd_i = nc.dram_tensor("d_agd_i", [TD, 128, I], BF16)
    d_agd_o = nc.dram_tensor("d_agd_o", [NCORES, TD, 128, I], BF16)
    d_ar = {r: (nc.dram_tensor(f"d_ar{r}_i", [1, 2], F32),
                nc.dram_tensor(f"d_ar{r}_o", [1, 2], F32)) for r in "ABCD"}
    d_kvx_i = nc.dram_tensor("d_kvx_i", [128, HP * VW], F32)
    d_kvx_o = nc.dram_tensor("d_kvx_o", [128, HP * VW], F32)
    d_red = nc.dram_tensor("d_red", [2, 128], F32)

    from contextlib import ExitStack
    es = ExitStack()

    def sbuf(name, shape, dt=F32):
        return es.enter_context(nc.sbuf_tensor(name, shape, dt))

    nA = max((KD * T + 1) // 2, 4 * T)
    arA = Arena(sbuf("aA", [128, nA], F32), nA)
    xnT = arA.bf16(KD * T)
    arA.reset(0)
    stq = [[arA.f32(T), arA.f32(T)] for _ in range(2)]

    nBv = (max(3 * KD * D, KI * D) + 1) // 2
    arB = Arena(sbuf("aB", [128, nBv], F32), nBv)
    WT = {w: arB.bf16(KD * D) for w in ("Wq", "Wk", "Wv")}
    arB.reset(0)
    wdT = arB.bf16(KI * D)

    WQW = max(D, I)
    nCv = (KD * D + 1) // 2 + max(
        2 * WQW + (KD * RUG + 1) // 2 + (KD * 128 + 1) // 2,
        (KD * T + 1) // 2) + 8
    arC = Arena(sbuf("aC", [128, nCv], F32), nCv)
    s_wno = arC.bf16(KD * D)
    _mkC = arC.off
    wq_ld = arC.f32(WQW)
    wq_u = arC.f32(WQW)
    s_shT = arC.bf16(KD * RUG)
    s_woS = arC.bf16(KD * 128)
    arC.reset(_mkC)
    x_qT = arC.bf16(KD * T)

    def seg1_alloc(ar):
        o = {}
        o["q_f"] = [ar.f32(D) for _ in range(2)]
        o["k_f"] = [ar.f32(D) for _ in range(2)]
        o["rq_f"] = [ar.f32(D) for _ in range(2)]
        o["rk_f"] = [ar.f32(D) for _ in range(2)]
        o["ctab"] = ar.f32(NCH * 32)
        o["stab"] = ar.f32(NCH * 32)
        o["s_xnb"] = ar.bf16(D)
        o["s_qb"] = ar.bf16(D)
        o["s_kb"] = ar.bf16(D)
        o["s_tq"] = ar.bf16(KD * 128)
        o["s_tk"] = ar.bf16(KD * 128)
        o["s_va"] = [ar.bf16(H * VW) for _ in range(2)]
        o["ck_bf"] = ar.bf16(HP * VW)
        return o

    def seg2_alloc(ar):
        o = {}
        o["s_q2"] = [ar.bf16(KD * 128) for _ in range(2)]
        o["s_k2"] = [ar.bf16(KD * 128) for _ in range(2)]
        o["s_v2"] = [ar.bf16(H * VW) for _ in range(2)]
        o["s_c2"] = [ar.bf16(HP * VW) for _ in range(2)]
        o["s_am"] = ar.bf16((H // 2) * 128)
        o["s_y"] = ar.bf16(D)
        o["s_yT"] = ar.bf16(KD * 128)
        o["GwB"] = ar.bf16(2 * T)
        if isinstance(o["GwB"], tuple):
            o["Gw"] = [o["GwB"]] * 2
        else:
            o["Gw"] = [o["GwB"][:, i * T:(i + 1) * T] for i in range(2)]
        o["s_upr"] = [ar.bf16(T) for _ in range(2)]
        o["s_wdn"] = ar.bf16(I)
        o["s_oq"] = [ar.bf16(KI * 128)]
        return o

    class _Count:
        def __init__(self):
            self.off = 0

        def f32(self, n):
            o = self.off
            self.off += n
            return ("f32", o, n)

        def bf16(self, n):
            n = n + (n % 2)
            o = self.off
            self.off += n // 2
            return ("bf16", o, n)

    c1, c2 = _Count(), _Count()
    seg1_alloc(c1)
    seg2_alloc(c2)
    nDv = max(c1.off, c2.off) + 4
    arD = Arena(sbuf("aD", [128, nDv], F32), nDv)
    g1 = seg1_alloc(arD)
    arD.reset(0)
    g2 = seg2_alloc(arD)
    q_f, k_f, rq_f, rk_f = g1["q_f"], g1["k_f"], g1["rq_f"], g1["rk_f"]
    ctab, stab = g1["ctab"], g1["stab"]
    s_xnb, s_qb, s_kb = g1["s_xnb"], g1["s_qb"], g1["s_kb"]
    s_tq, s_tk, s_va = g1["s_tq"], g1["s_tk"], g1["s_va"]
    ck_bf = g1["ck_bf"]
    s_q2, s_k2, s_v2, s_c2 = g2["s_q2"], g2["s_k2"], g2["s_v2"], g2["s_c2"]
    s_am, s_y, s_yT = g2["s_am"], g2["s_y"], g2["s_yT"]
    Gw, s_upr, s_wdn, s_oq = g2["Gw"], g2["s_upr"], g2["s_wdn"], g2["s_oq"]
    s_oq2 = g2["GwB"][:, 0:KI * 128]
    s_ws_t = sbuf("s_wsded", [128, 2 * KD * 128], BF16)
    s_ws = [s_ws_t.ap()[:, i * KD * 128:(i + 1) * KD * 128] for i in range(2)]
    s_ev_t = sbuf("s_evded", [128, 4 * 512], BF16)
    s_evB = s_ev_t.ap()
    s_ev = [s_evB[:, i * 512:(i + 1) * 512] for i in range(4)]

    a_E = sbuf("aE", [128, 2 * D], F32)
    xst = [a_E.ap()[:, i * D:(i + 1) * D] for i in range(2)]

    identb = sbuf("sb_identb", [128, 128], BF16)
    identf = sbuf("sb_identf", [128, 128], F32)
    stG_bf = sbuf("stG_bf", [128, HP * VW], BF16)
    ones_row = sbuf("ones_row", [1, 128], F32)
    s_fr = sbuf("s_fr", [1, 8], F32)
    s_r2 = sbuf("s_r2", [2, 1], F32)
    maskT = sbuf("maskT", [128, HB * 128], F32)
    s_yf = sbuf("s_yf", [128, (H // 2) * DH], F32)
    state = sbuf("state", [128, HP * VW], F32)
    st_bf = state[:].bitcast(BF16)[:, 0:HP * VW]
    stG = sbuf("stG", [128, HP * VW], F32)
    s_rc = sbuf("s_rc", [128, H], F32)
    pUG = sbuf("pUG", [128, max(2 * KI, NCH)], F32)
    pmax = pUG[:, 0:NCH]
    pPO = sbuf("pPO", [128, 2 * KI], F32)
    red1 = sbuf("red1", [1, 4], F32)
    redp = sbuf("redp", [128, 4], F32)
    rowB = sbuf("rowred", [1, 128], F32)
    row = rowB[0:1, 0:64]
    row2 = rowB[0:1, 64:128]
    gmx = sbuf("gmx", [128, 8], F32)
    scl = sbuf("scl", [128, 18], F32)
    flg = sbuf("flg", [128, 2], F32)
    NBmax = max(D // 32, I // 32)
    wbm = sbuf("wbm", [128, NBmax], F32)
    whs = sbuf("whs", [128, NBmax], F32)
    nst = 6 * ((D + 511) // 512)
    s_bn = sbuf("s_bn", [128, nst], F32)
    s_mv = sbuf("s_mv", [128, 2], F32)

    (S_SX, S_INVX, S_SU, S_INVU, S_SG, S_INVG,
     S_S3, S_C3, S_SO, S_C5, S_T0, S_T1, S_T2, S_NCX, S_BSG,
     S_MG, S_NMG) = range(17)

    ps = [es.enter_context(nc.psum_tensor(f"ps{i}", [128, 512], F32)) for i in range(6)]
    pt = [es.enter_context(nc.psum_tensor(f"pt{i}", [128, 1024], BF16)) for i in range(2)]

    P = Prog()
    sems = {s: es.enter_context(nc.semaphore(f"sem_{s}")) for s in P.semnames}

    def TT(eng, out, i0, i1, op, deps, tag, inc=1):
        return P.op(eng, lambda e: e.tensor_tensor(out, i0, i1, op), deps, inc, tag)

    def TS(eng, out, i0, s1, s2, o0, o1, deps, tag, inc=1):
        if s2 is None:
            return P.op(eng, lambda e: e.tensor_single_scalar(out, i0, s1, o0),
                        deps, inc, tag)
        return P.op(eng, lambda e: e.tensor_scalar(out, i0, s1, s2, o0, o1),
                    deps, inc, tag)

    def STT(eng, out, i0, s, i1, o0, o1, deps, tag, inc=1):
        return P.op(eng, lambda e: e.scalar_tensor_tensor(out, i0, s, i1, o0, o1),
                    deps, inc, tag)

    def RED(eng, out, in_, op, deps, tag, absv=False, inc=1):
        return P.op(eng, lambda e: e.tensor_reduce(out, in_, AX, op,
                                                   apply_absolute_value=absv),
                    deps, inc, tag)

    def CP(eng, out, in_, deps, tag, inc=1):
        if eng == "sc":
            return P.op(eng, lambda e: e.copy(out, in_), deps, inc, tag)
        return P.op(eng, lambda e: e.tensor_copy(out, in_), deps, inc, tag)

    def RECIP(out, in_, deps, tag):
        return P.op("ve", lambda e: e.reciprocal(out, in_), deps, 1, tag)

    def ACT(out, in_, func, bias, scale, deps, tag, inc=1):
        return P.op("sc", lambda e: e.activation(out, in_, func, bias=bias,
                                                 scale=scale), deps, inc, tag)

    def MM(out, lhsT, rhs, start, stop, deps, tag, inc=0, tile=None):
        return P.op("pe", lambda e: e.matmul(out, lhsT, rhs, start=start,
                                             stop=stop, tile_position=tile),
                    deps, inc, tag)

    def TRN(out, in_, deps, tag, inc=0):
        return P.op("pe", lambda e: e.transpose(out, in_, identb[:]), deps, inc, tag)

    def CC(kind, op, groups, ins, outs, deps, tag):
        return P.op("gp", lambda e: e.collective_compute(kind, op,
                                                         replica_groups=groups,
                                                         ins=ins, outs=outs),
                    deps, 1, tag)

    def cover(toks):
        best = {}
        for t in toks:
            if t is None:
                continue
            s, v = t
            if best.get(s, 0) < v:
                best[s] = v
        return [(s, v) for s, v in best.items()]

    def xpart_max(cols_ap, ncols, out_row_ap, bank, deps, tag):
        t1 = P.op("pe", lambda e: e.transpose(bank[0:ncols, 0:128], cols_ap,
                                              identf[:]),
                  list(deps) + [tld_identf], 1, tag + ".tp")
        t2 = RED("ve", out_row_ap, bank[0:ncols, 0:128], A.max, [t1], tag + ".rd")
        return t2

    def bcast_mm(row_ap, n, dst_ap, bank, deps, tag):
        t1 = MM(bank[:, 0:n], ones_row[0:1, 0:128], row_ap, True, True,
                list(deps) + [t_ones], tag + ".mm", inc=1)
        t2 = CP("ve", dst_ap, bank[:, 0:n], [t1], tag + ".cp")
        return t2

    tp_state = dict(g=0, evtok=[])

    def transpose_group(srcs, dst, deps, tag, evac_deps=(), dst3=False):
        g = tp_state["g"]
        tp_state["g"] += 1
        bank = pt[g % 2]
        d0 = list(deps) + [tld_identb]
        if g >= 2:
            d0.append(tp_state["evtok"][g - 2])
        tpe = None
        for bi, s in enumerate(srcs):
            tpe = TRN(bank[:, bi * 128:(bi + 1) * 128], s,
                      d0 if bi == 0 else [], f"{tag}.t{bi}",
                      inc=(1 if bi == len(srcs) - 1 else 0))
        src_ap = bank[:, 0:len(srcs) * 128]
        if dst3:
            src_ap = src_ap.rearrange("p (k d) -> p k d", d=128)
        te = CP("sc", dst, src_ap, [tpe] + list(evac_deps),
                f"{tag}.ev")
        tp_state["evtok"].append(te)
        return te


    def scale_from_max(gcol, s_col, inv_col, qdiv, deps, tag):
        a = RECIP(scl[:, S_T2:S_T2 + 1], gcol, deps, tag + ".r1")
        b = TS("ve", scl[:, S_T2:S_T2 + 1], scl[:, S_T2:S_T2 + 1], qdiv, None,
               A.mult, None, [a], tag + ".m")
        c0 = RECIP(s_col, scl[:, S_T2:S_T2 + 1], [b], tag + ".r2")
        d = TS("ve", s_col, s_col, 1e-12, None, A.max, None, [c0], tag + ".mx")
        e0 = RECIP(inv_col, s_col, [d], tag + ".ri") if inv_col is not None else d
        return d, e0


    def emit_tail():
        deps = [(s, P.count[s]) for s in P.semnames if P.count[s] > 0]
        td = P.dma("sc", out_hbm.ap()[0:C, :], xst[0], deps=deps, tag="tail.out")
        P.op("gp", lambda e: e.nop(), [td], 1, "fin")
        P.verify()
        blk = es.enter_context(nc.Block())
        for _e, _attr in (("sy", "sync"), ("sc", "scalar"), ("ve", "vector"),
                          ("gp", "gpsimd"), ("pe", "tensor")):
            getattr(blk, _attr)(lambda eng, _e=_e: P.emit_engine(_e, eng, sems))
        es.close()
        return nc

    tld_identb = P.dma("sy", identb[:], identb_in.ap(), tag="ld.identb")
    tld_identf = P.dma("sy", identf[:], identf_in.ap(), tag="ld.identf")
    t_ones = P.op("ve", lambda e: e.memset(ones_row[0:1, :], 1.0), (), 1, "ones_row")
    tld_mask = P.dma("sy", maskT[:], mask_in.ap(), tag="ld.mask")
    tld_ctab = P.dma("sy", ctab, ctab_in.ap(), tag="ld.ctab")
    tld_stab = P.dma("sy", stab, stab_in.ap(), tag="ld.stab")
    tld_fr = P.dma("sy", s_fr[0:1, :], flags_in.ap(), tag="ld.fr")
    tld_flg = bcast_mm(s_fr[0:1, 0:2], 2, flg[:, 0:2], ps[0], [tld_fr], "flgb")
    t_state0 = P.op("ve", lambda e: e.memset(state[:], 0.0), (), 1, "state0")
    t_mg0 = P.op("ve", lambda e: e.memset(scl[:, S_MG:S_MG + 1], MAGIC),
                 (), 1, "mgc")
    t_mg = P.op("ve", lambda e: e.memset(scl[:, S_NMG:S_NMG + 1], -MAGIC),
                [t_mg0], 1, "nmgc")
    t_vaones = []
    for i in range(2):
        t_vaones.append(P.op(
            "ve",
            lambda e, i=i: e.memset(
                s_va[i].rearrange("p (h w) -> p h w", w=VW)[:, :, DH:DH + 1], 1.0),
            (), 1, f"vaones{i}"))

    if stage <= 1:
        return emit_tail()
    t_wT = {}
    prev_tok = None
    woS_guard = None
    for w in ("Wq", "Wk", "Wv", "Wo"):
        t_cast = P.dma(
            "gp", s_wno.rearrange("p (k m) -> p k m", k=KD),
            W_in[w].ap().rearrange("(k p) m -> p k m", p=128),
            deps=([] if prev_tok is None else [prev_tok]), tag=f"cast.{w}")
        last = None
        for ic in range(KD):
            srcs = [s_wno[:, oc * D + ic * 128:oc * D + ic * 128 + 128]
                    for oc in range(KD)]
            if w != "Wo":
                last = transpose_group(srcs, WT[w][:, ic * D:ic * D + KD * 128],
                                       [t_cast], f"wT.{w}.{ic}")
            else:
                ev = transpose_group(srcs, s_woS, [t_cast], f"wT.Wo.{ic}",
                                     evac_deps=([woS_guard] if woS_guard else []))
                last = P.dma("sy", d_woT.ap()[:, ic * KD * 128:(ic + 1) * KD * 128],
                             s_woS, deps=[ev], tag=f"woT.sp{ic}")
                woS_guard = last
        t_wT[w] = last
        prev_tok = last

    t_agu_in_holder = [None]

    if stage <= 2:
        return emit_tail()
    nwt = 2 * TUG + TD
    t_shard = {}
    t_agd_in = []
    prev_use = None
    for t in range(nwt):
        wid = D if t < 2 * TUG else I
        nb = wid // 32
        if t < TUG:
            src_h = wup_in.ap()[(t % TUG) * 128:(t % TUG) * 128 + 128, :]
        elif t < 2 * TUG:
            u = t - TUG
            src_h = wg_in.ap()[u * 128:u * 128 + 128, :]
        else:
            u = t - 2 * TUG
            src_h = wd_in.ap()[u * 128:u * 128 + 128, :]
        tl = P.dma("sy", wq_ld[:, 0:wid], src_h,
                   deps=([prev_use] if prev_use else []), tag=f"wq.ld{t}")
        src3 = wq_ld[:, 0:wid].rearrange("p (b j) -> p b j", j=32)
        t_bm = RED("ve", wbm[:, 0:nb], src3, A.max, [tl], f"wq.bm{t}", absv=True)
        ta = RECIP(whs[:, 0:nb], wbm[:, 0:nb], [t_bm], f"wq.sa{t}")
        tb = TS("ve", whs[:, 0:nb], whs[:, 0:nb], WQMAX, None, A.mult, None,
                [ta], f"wq.sb{t}")
        tc = RECIP(wbm[:, 0:nb], whs[:, 0:nb], [tb], f"wq.sc{t}")
        t_s = TS("ve", wbm[:, 0:nb], wbm[:, 0:nb], 1e-12, None, A.max, None,
                 [tc], f"wq.s{t}")
        t_h = RECIP(whs[:, 0:nb], wbm[:, 0:nb], [t_s], f"wq.h{t}")
        bss = wbm[:, 0:nb].rearrange("p (b o) -> p b o", o=1).broadcast_to([128, nb, 32])
        bhs = whs[:, 0:nb].rearrange("p (b o) -> p b o", o=1).broadcast_to([128, nb, 32])
        u3 = wq_u[:, 0:wid].rearrange("p (b j) -> p b j", j=32)
        t_u = TT("gp", u3, src3, bhs, A.mult, [t_h], f"wq.u{t}")
        t_u2 = TS("ve", wq_u[:, 0:wid], wq_u[:, 0:wid], MAGIC, None, A.add,
                  None, [t_u], f"wq.u2{t}")
        t_u3 = TS("ve", wq_u[:, 0:wid], wq_u[:, 0:wid], -MAGIC, None, A.add,
                  None, [t_u2], f"wq.u3{t}")
        dst_bf = wq_ld[:, 0:wid].bitcast(BF16)[:, 0:wid]
        t_q = TT("gp", dst_bf.rearrange("p (b j) -> p b j", j=32), u3, bss,
                 A.mult, [t_u3], f"wq.q{t}")
        if t < 2 * TUG:
            u = t % TUG
            last = None
            for ic in range(KD):
                src = dst_bf[:, ic * 128:(ic + 1) * 128]
                dstp = s_shT[:, ic * RUG + u * 128:ic * RUG + u * 128 + 128]
                extra = []
                if t == TUG:
                    extra = [t_agu_in_holder[0]]
                last = transpose_group([src], dstp, [t_q] + extra, f"shT.{t}.{ic}")
            t_shard[t] = last
            prev_use = last
        else:
            u = t - 2 * TUG
            td = P.dma("gp", d_agd_i.ap()[u], dst_bf, deps=[t_q], tag=f"agd.i{t}")
            t_agd_in.append(td)
            prev_use = td
        if t == TUG - 1:
            t_agu_in_holder[0] = P.dma(
                "gp", d_agu_i.ap().rearrange("(k p) m -> p k m", p=128),
                s_shT.rearrange("p (k m) -> p k m", k=KD),
                deps=[t_shard[tt2] for tt2 in range(TUG)], tag="agu.i")
            prev_use = prev_use
    t_agu_in = t_agu_in_holder[0]
    t_ag_up = CC("AllGather", A.bypass, [list(range(NCORES))],
                 [d_agu_i.ap().opt()], [d_agu_o.ap().opt()], [t_agu_in], "ag.up")
    t_agg_in = P.dma("gp", d_agg_i.ap().rearrange("(k p) m -> p k m", p=128),
                     s_shT.rearrange("p (k m) -> p k m", k=KD),
                     deps=[t_shard[TUG + tt2] for tt2 in range(TUG)], tag="agg.i")
    t_ag_g = CC("AllGather", A.bypass, [list(range(NCORES))],
                [d_agg_i.ap().opt()], [d_agg_o.ap().opt()], [t_agg_in], "ag.gt")
    t_ag_d = CC("AllGather", A.bypass, [list(range(NCORES))],
                [d_agd_i.ap().opt()], [d_agd_o.ap().opt()], t_agd_in, "ag.dn")

    if stage <= 3:
        return emit_tail()
    t_ln = []
    t_xnT = []
    for c in range(NCH):
        deps = [t_ln[c - 2]] if c >= 2 else []
        tl = P.dma("sy", xst[c % 2], x_in.ap()[c * C:(c + 1) * C, :], deps=deps,
                   tag=f"ln.x{c}")
        nsl = (D + 511) // 512
        t1 = None
        for sl in range(nsl):
            w0 = min(512, D - sl * 512)
            t1 = P.op("ve", lambda e, c=c, sl=sl, w0=w0: e.bn_stats(
                s_bn[:, sl * 6:(sl + 1) * 6], xst[c % 2][:, sl * 512:sl * 512 + w0]),
                [tl] if sl == 0 else [], 1, f"ln.bs{c}.{sl}")
        t2 = P.op("ve", lambda e: e.bn_aggr(s_mv[:], s_bn[:]), [t1], 1, f"ln.ba{c}")
        t3 = TS("ve", s_mv[:, 1:2], s_mv[:, 1:2], EPS_LN, None, A.add, None,
                [t2], f"ln.ve{c}")
        t4 = ACT(s_mv[:, 1:2], s_mv[:, 1:2], AF.Sqrt, 0.0, 1.0, [t3], f"ln.sq{c}")
        t5 = RECIP(s_mv[:, 1:2], s_mv[:, 1:2], [t4], f"ln.rc{c}")
        t6 = TS("ve", rq_f[c % 2], xst[c % 2], s_mv[:, 0:1], s_mv[:, 1:2],
                A.subtract, A.mult,
                [t5] + ([t_xnT[c - 2]] if c >= 2 else []), f"ln.ap{c}")
        t_ln.append(t6)
        t7 = RED("ve", pmax[:, c:c + 1], rq_f[c % 2], A.max, [t6], f"ln.mx{c}",
                 absv=True)
        t8 = CP("ve", s_xnb, rq_f[c % 2], [t7], f"ln.cb{c}")
        srcs = [s_xnb[:, ic * 128:(ic + 1) * 128] for ic in range(KD)]
        dst = xnT.rearrange("p (k t) -> p k t", k=KD)[:, :, c * 128:(c + 1) * 128]
        t_xnT.append(transpose_group(srcs, dst, [t8], f"xnT.{c}"))
    t_cmb = RED("ve", redp[:, 0:1], pmax[:, 0:NCH], A.max, [t_ln[NCH - 1]], "xa.r1")
    t_gx = xpart_max(redp[:, 0:1], 1, red1[0:1, 0:1], ps[0],
                     [t_cmb, tld_flg], "xa")
    t_arA_w = P.dma("gp", d_ar["A"][0].ap()[0:1, 0:1], red1[0:1, 0:1],
                    deps=[t_gx], tag="arA.w")
    t_arA = CC("AllReduce", A.max, [list(range(NCORES))],
               [d_ar["A"][0].ap()[0:1, 0:1]], [d_ar["A"][1].ap()[0:1, 0:1]],
               [t_arA_w], "arA")
    t_arA_r = P.dma("gp", row[0:1, 0:1], d_ar["A"][1].ap()[0:1, 0:1],
                    deps=[t_arA], tag="arA.r")

    assert 2 * D >= T
    wide2 = a_E.ap()[:, 0:T]
    t_gmx0 = bcast_mm(row[0:1, 0:1], 1, gmx[:, 0:1], ps[0], [t_arA_r, t_gx],
                      "arA.b")
    t_sx, t_invx = scale_from_max(
        gmx[:, 0:1], scl[:, S_SX:S_SX + 1], scl[:, S_INVX:S_INVX + 1],
        QMAX, [t_gmx0], "sx")
    t_ncx = TS("ve", scl[:, S_NCX:S_NCX + 1], scl[:, S_SX:S_SX + 1], -MAGIC,
               None, A.mult, None, [t_sx], "ncx")
    xq_dead = [t_agu_in, t_agg_in, t_wT["Wo"], t_mg] + cover(t_agd_in)
    t_xq = None
    for ic in range(KD):
        sl = x_qT[:, ic * T:(ic + 1) * T]
        d0 = [t_invx, t_ncx, t_mg, t_ln[-1], t_ln[-2]] + cover(t_xnT)
        if ic == 0:
            d0 += xq_dead
        if t_xq is not None:
            d0.append(t_xq)
        t1 = ACT(wide2, xnT[:, ic * T:(ic + 1) * T], AF.Identity,
                 scl[:, S_MG:S_MG + 1], scl[:, S_INVX:S_INVX + 1], d0,
                 f"xq.y{ic}")
        t_xq = ACT(sl, wide2, AF.Identity, scl[:, S_NCX:S_NCX + 1],
                   scl[:, S_SX:S_SX + 1], [t1], f"xq.s{ic}")
    t_xq_done = t_xq

    if stage <= 4:
        return emit_tail()
    xnT3 = xnT.rearrange("p (k t) -> p k t", k=KD)
    t_evq, t_evk, t_evv = [], [], []
    t_ropeq, t_ropek = [], []
    t_cast_q, t_cast_k = [], []
    t_spq, t_spk, t_spv, t_ck = [], [], [], []
    t_stup = [t_state0]
    qkv_toks = []
    mm_prev2 = [None, None]
    ev_prev = [None]
    spill_toks = []
    absmax_toks = []

    def qkv_mms(c):
        deps0 = [t_xnT[c], t_wT["Wq"], t_wT["Wk"], t_wT["Wv"], t_stup[-1]]
        if c == 0:
            deps0.append(t_gmx0)
        if c >= 1:
            deps0 += [t_evq[c - 1], t_evk[c - 1], t_evv[c - 1]]
        if ev_prev[0] is not None:
            deps0.append(ev_prev[0])
        out = []
        for ic in range(KD):
            lhs = xnT3[:, ic, c * 128:(c + 1) * 128]
            for pi, w in enumerate(("Wq", "Wk", "Wv")):
                for hh in range(NBK):
                    bank = ps[pi * NBK + hh]
                    rhs = WT[w][:, ic * D + hh * PW:ic * D + hh * PW + PW]
                    tk3 = MM(bank[:, 0:PW], lhs, rhs, ic == 0, ic == KD - 1,
                             deps0 if (ic == 0 and pi == 0 and hh == 0) else [],
                             f"qkv{c}.{ic}.{pi}.{hh}",
                             inc=(1 if ic == KD - 1 else 0))
                    if ic == KD - 1 and hh == NBK - 1:
                        out.append(tk3)
        qkv_toks.append(tuple(out))

    cw = 32

    def rope_phi(src_f, dst_f, scr_f, dep_ev, pre, tag):
        c = int(tag.split("_")[-1])
        cs = ctab[:, c * cw:(c + 1) * cw].rearrange("p (o j) -> p o j", o=1).broadcast_to([128, H, cw])
        sn = stab[:, c * cw:(c + 1) * cw].rearrange("p (o j) -> p o j", o=1).broadcast_to([128, H, cw])
        s3 = src_f.rearrange("p (h d) -> p h d", d=DH)
        o3 = dst_f.rearrange("p (h d) -> p h d", d=DH)
        sc3 = scr_f.rearrange("p (h d) -> p h d", d=DH)
        q1, q2 = s3[:, :, 0:cw], s3[:, :, cw:2 * cw]
        a1 = TT("gp", o3[:, :, 0:cw], q1, cs, A.mult,
                [dep_ev, tld_ctab, tld_stab] + pre, tag + ".a1")
        a2 = TT("gp", sc3[:, :, 0:cw], q2, sn, A.mult, [dep_ev] + pre, tag + ".a2")
        a3 = TT("gp", o3[:, :, 0:cw], o3[:, :, 0:cw], sc3[:, :, 0:cw],
                A.subtract, [a1, a2], tag + ".a3")
        b1 = TT("gp", o3[:, :, cw:2 * cw], q1, sn, A.mult, [dep_ev], tag + ".b1")
        b2 = TT("gp", sc3[:, :, cw:2 * cw], q2, cs, A.mult, [a2], tag + ".b2")
        b3 = TT("gp", o3[:, :, cw:2 * cw], o3[:, :, cw:2 * cw], sc3[:, :, cw:2 * cw],
                A.add, [b1, b2], tag + ".b3")
        p1 = TS("ve", scr_f, dst_f, 0.0, None, A.min, None, [a3, b3], tag + ".p1")
        p2 = ACT(scr_f, scr_f, AF.Exp, 0.0, 1.0, [p1], tag + ".p2")
        p3 = TS("ve", dst_f, dst_f, 0.0, None, A.max, None, [p1], tag + ".p3")
        p4 = TT("gp", dst_f, dst_f, scr_f, A.add, [p2, p3], tag + ".p4")
        return p4

    def pass1_evacs(c):
        tq, tk_, tv = qkv_toks[c]
        dq = [tq] + ([t_ropek[c - 2]] if c >= 2 else [])
        dk = [tk_] + ([t_ropek[c - 2]] if c >= 2 else [])
        dv = [tv, t_vaones[c % 2]]
        if c >= 2:
            dv += [t_spv[c - 2]]
        if c >= 1:
            dv.append(t_stup[-1])
        par = c % 2
        eq = ek = ev = None
        for hh in range(NBK):
            eq = CP("sc", q_f[par][:, hh * PW:hh * PW + PW], ps[0 * NBK + hh][:, 0:PW],
                    dq if hh == 0 else [], f"evq{c}.{hh}")
        for hh in range(NBK):
            ek = CP("sc", k_f[par][:, hh * PW:hh * PW + PW], ps[1 * NBK + hh][:, 0:PW],
                    dk if hh == 0 else [], f"evk{c}.{hh}")
        va3 = s_va[c % 2].rearrange("p (h w) -> p h w", w=VW)
        nhh = PW // DH
        for hh in range(NBK):
            ev = CP("sc", va3[:, hh * nhh:(hh + 1) * nhh, 0:DH],
                    ps[2 * NBK + hh][:, 0:PW].rearrange("p (h w) -> p h w", w=DH),
                    dv if hh == 0 else [], f"evv{c}.{hh}")
        t_evq.append(eq)
        t_evk.append(ek)
        t_evv.append(ev)

    def pass1_rope(c):
        par = c % 2
        rq = rope_phi(q_f[par], rq_f[par], rk_f[par], t_evq[c],
                      [t_ropek[c - 2]] if c >= 2 else [], f"rq_{c}")
        t_ropeq.append(rq)
        rk = rope_phi(k_f[par], rk_f[par], q_f[par], t_evk[c], [rq], f"rk_{c}")
        t_ropek.append(rk)

    t_gq, t_gk = [], []

    def pass1_tail(c):
        cq = CP("ve", s_qb, rq_f[c % 2],
                [t_ropeq[c]] + ([t_gq[c - 1]] if c >= 1 else []), f"castq{c}")
        ckk = CP("ve", s_kb, rk_f[c % 2],
                 [t_ropek[c]] + ([t_gk[c - 1], t_stup[-1]] if c >= 1 else []),
                 f"castk{c}")
        t_cast_q.append(cq)
        t_cast_k.append(ckk)
        gq = transpose_group([s_qb[:, ic * 128:(ic + 1) * 128] for ic in range(KD)],
                             s_tq, [cq],
                             f"tq{c}", evac_deps=([t_spq[c - 1]] if c >= 1 else []))
        gk = transpose_group([s_kb[:, ic * 128:(ic + 1) * 128] for ic in range(KD)],
                             s_tk, [ckk],
                             f"tk{c}", evac_deps=([t_spk[c - 1]] if c >= 1 else []))
        t_gq.append(gq)
        t_gk.append(gk)
        t_spq.append(P.dma("sy", d_qT.ap()[c], s_tq, deps=[gq], tag=f"spq{c}"))
        t_spk.append(P.dma("sy", d_kT.ap()[c], s_tk, deps=[gk], tag=f"spk{c}"))
        t_spv.append(P.dma("sy", d_va.ap()[c], s_va[c % 2], deps=[t_evv[c]],
                           tag=f"spv{c}"))
        bank_free = t_evv[c + 1] if c + 1 < len(t_evv) else t_evv[c]
        kvdep = [t_cast_k[c], t_evv[c], bank_free, t_stup[-1]]
        kv_toks = []
        for h in range(H):
            j = h // 2
            bank = ps[2 * NBK + (0 if j < PB else 1)]
            colbase = (j % PB) * VW
            outap = bank[(h % 2) * 64:(h % 2) * 64 + 64, colbase:colbase + VW]
            rhs = s_va[c % 2][:, h * VW:(h + 1) * VW]
            tkv = MM(outap, s_kb[:, h * DH:(h + 1) * DH], rhs,
                     True, True, kvdep if h == 0 else [],
                     f"kv{c}.{h}", inc=(1 if h % 2 == 1 else 0),
                     tile=((0, 64) if h % 2 == 1 else None))
            if h % 2 == 1:
                kv_toks.append(tkv)
        tck = CP("ve", ck_bf, state[:],
                 [t_stup[-1]] + ([t_ck[c - 1]] if c >= 1 else []), f"ck{c}")
        t_ck.append(P.dma("sy", d_ck.ap()[c], ck_bf, deps=[tck], tag=f"ckw{c}"))
        upds = None
        for b in range(NKB):
            lo = b * PB * VW
            hi = min(HP, (b + 1) * PB) * VW
            upds = TT("ve", state[:, lo:hi], ps[2 * NBK + b][:, 0:hi - lo],
                      state[:, lo:hi], A.add,
                      [kv_toks[min(len(kv_toks) - 1, (b + 1) * PB - 1)], tck],
                      f"stup{c}.{b}")
        t_stup.append(upds)

    spill_toks = []
    absmax_toks = []
    ffn_parts = [(0, d_up, d_agu_o, t_ag_up, "up"),
                 (1, d_gt, d_agg_o, t_ag_g, "gt")]

    def ffn_block(idx):
        u, dsp, ag_o, t_ag, pref = ffn_parts[idx // KI]
        Ic = idx % KI
        r, uu = divmod(Ic, TUG)
        r = r % NCORES
        wdeps = [t_ag]
        if idx >= 2:
            wdeps.append(mm_prev2[idx % 2])
        twl = P.dma("sy", s_ws[idx % 2].rearrange("p (k m) -> p k m", k=KD),
                    ag_o.ap()[r, :, :, uu * 128:(uu + 1) * 128].rearrange(
                        "k p m -> p k m"),
                    deps=wdeps, tag=f"{pref}.w{Ic}")
        mm_last = None
        for ic in range(KD):
            lhs = s_ws[idx % 2][:, ic * 128:(ic + 1) * 128]
            for tt_ in range(NT):
                bank = ps[tt_ % 4]
                d0 = []
                if ic == 0 and tt_ == 0:
                    d0 = [twl, t_xq_done]
                    if ev_prev[0] is not None:
                        d0.append(ev_prev[0])
                    if t_evq:
                        d0 += [t_evq[-1], t_evk[-1]]
                rhs = x_qT[:, ic * T + tt_ * W5:ic * T + tt_ * W5 + W5]
                mm_last = MM(bank[:, 0:W5], lhs, rhs, ic == 0, ic == KD - 1,
                             d0, f"{pref}.mm{Ic}.{ic}.{tt_}",
                             inc=(1 if ic == KD - 1 else 0))
        mm_prev2[idx % 2] = mm_last
        ev_toks_ic = []
        g0 = len(spill_toks) % 4
        for tt_ in range(NT):
            g = len(spill_toks)
            evd = [mm_last]
            if g >= 4:
                evd += [spill_toks[g - 4], absmax_toks[g - 4]]
            tev = CP("sc", s_ev[g % 4][:, 0:W5], ps[tt_ % 4][:, 0:W5], evd,
                     f"{pref}.ev{Ic}.{tt_}")
            ev_toks_ic.append(tev)
            tsp = P.dma("sy", dsp.ap()[Ic, :, tt_ * W5:tt_ * W5 + W5],
                        s_ev[g % 4][:, 0:W5], deps=[tev],
                        tag=f"{pref}.sp{Ic}.{tt_}")
            spill_toks.append(tsp)
            ev_prev[0] = tev
        tmx = RED("ve", pUG[:, u * KI + Ic:u * KI + Ic + 1],
                  s_evB[:, g0 * W5:(g0 + NT) * W5], A.max, ev_toks_ic,
                  f"{pref}.mx{Ic}", absv=True)
        absmax_toks.extend([tmx] * NT)

    NBLK = 2 * KI
    nffn = 0
    qkv_mms(0)
    pass1_evacs(0)
    pass1_rope(0)
    for c in range(NCH):
        if c + 1 < NCH:
            qkv_mms(c + 1)
            pass1_evacs(c + 1)
        pass1_tail(c)
        tgt = 0 if c < 2 else ((c - 1) * NBLK) // (NCH - 2)
        while nffn < tgt:
            ffn_block(nffn)
            nffn += 1
        if c + 1 < NCH:
            pass1_rope(c + 1)
    while nffn < NBLK:
        ffn_block(nffn)
        nffn += 1

    if stage <= 5:
        return emit_tail()
    t_ctb = TS("ve", stG[:], state[:], flg[:, 0:1], None, A.mult, None,
               [t_stup[-1], tld_flg], "kvx.ctb")
    t_kvw = P.dma("gp", d_kvx_i.ap(), stG[:], deps=[t_ctb], tag="kvx.w")
    t_kvx = CC("AllReduce", A.add,
               [[2 * i, 2 * i + 1] for i in range(NCORES // 2)],
               [d_kvx_i.ap()], [d_kvx_o.ap()], [t_kvw], "kvx")
    t_kvr = P.dma("gp", stG[:], d_kvx_o.ap(), deps=[t_kvx], tag="kvx.r")
    t_gmask0 = TS("ve", stG[:], stG[:], flg[:, 1:2], None, A.mult, None,
                  [t_kvr], "kvx.m")
    t_gmask = CP("ve", stG_bf[:], stG[:], [t_gmask0], "kvx.cb")

    last_qkv = t_evv[NCH - 1]
    pass1_dead = [t_cast_q[-1], t_cast_k[-1], t_stup[-1], t_ctb,
                  t_gmask, tp_state["evtok"][-1]] + cover(
                      t_ck + t_spv + t_spq + t_spk +
                      list(qkv_toks[-1]))

    if stage <= 6:
        return emit_tail()
    t_rB1 = RED("ve", redp[:, 1:2], pUG[:, 0:KI], A.max, absmax_toks, "rB.u")
    t_rB2 = RED("ve", redp[:, 2:3], pUG[:, KI:2 * KI], A.max,
                absmax_toks, "rB.g")
    t_gB = xpart_max(redp[:, 1:3], 2, s_r2[0:2, 0:1], ps[0],
                     [t_rB1, t_rB2, t_gmx0], "rB")
    t_arBw = P.dma("gp", d_ar["B"][0].ap().rearrange("a p -> p a"),
                   s_r2.ap(), deps=[t_gB], tag="arB.w")
    t_arB = CC("AllReduce", A.max, [list(range(NCORES))],
               [d_ar["B"][0].ap()], [d_ar["B"][1].ap()], [t_arBw], "arB")
    t_arB_r = P.dma("gp", row[0:1, 0:2], d_ar["B"][1].ap()[0:1, 0:2],
                    deps=[t_arB], tag="arB.r")
    t_gmxB = bcast_mm(row[0:1, 0:2], 2, gmx[:, 1:3], ps[0],
                      [t_arB_r, t_gB], "arB.b")
    t_su, t_invu = scale_from_max(
        gmx[:, 1:2], scl[:, S_SU:S_SU + 1], scl[:, S_INVU:S_INVU + 1],
        QMAX, [t_gmxB], "su")
    t_sg, t_invg = scale_from_max(
        gmx[:, 2:3], scl[:, S_SG:S_SG + 1], scl[:, S_INVG:S_INVG + 1],
        QMAX, [t_gmxB], "sg")
    t_bsg = TS("ve", scl[:, S_BSG:S_BSG + 1], scl[:, S_SG:S_SG + 1], -MAGIC,
               None, A.mult, None, [t_sg], "bsg")

    if stage <= 7:
        return emit_tail()
    t_woT_ld = P.dma("sy", s_wno, d_woT.ap(), deps=[t_wT["Wo"], t_agg_in],
                     tag="woT.ld")

    t_wdT_groups = []

    def emit_wdT(tile_idx):
        r, uu = divmod(tile_idx, TD)
        r = r % NCORES
        deps = [t_ag_d, last_qkv] + (pass1_dead if tile_idx == 0 else [])
        if tile_idx >= 1:
            deps.append(t_wdT_groups[-1])
        tl = P.dma("sy", s_wdn, d_agd_o.ap()[r, uu], deps=deps,
                   tag=f"wdT.ld{tile_idx}")
        last = None
        o = tile_idx * 128
        wdT3 = wdT.rearrange("p (k d) -> p k d", d=D)
        for g0 in range(0, KI, 8):
            ng = min(8, KI - g0)
            srcs = [s_wdn[:, (g0 + i) * 128:(g0 + i + 1) * 128]
                    for i in range(ng)]
            dst = wdT3[:, g0:g0 + ng, o:o + 128]
            last = transpose_group(srcs, dst, [tl], f"wdT.{tile_idx}.{g0}",
                                   dst3=True)
        t_wdT_groups.append(last)

    t_p2_ld = []
    t_y = []
    t_out = []
    prev_am = [t_gmxB]
    prev_yf = [None]
    prev_numev = [None]
    prev_num_mm = [None]
    prev_wo = [None]

    def emit_pass2_chunk(c):
        rot = c % 2
        extra = pass1_dead if c < 2 else [t_p2_ld[c - 2][0]]
        tq2 = P.dma("sy", s_q2[rot], d_qT.ap()[c], deps=[t_spq[c]] + extra,
                    tag=f"p2q{c}")
        tk2 = P.dma("sy", s_k2[rot], d_kT.ap()[c],
                    deps=[t_spk[c]] + (pass1_dead if c < 2 else [t_p2_ld[c - 2][1]]),
                    tag=f"p2k{c}")
        tv2 = P.dma("sy", s_v2[rot], d_va.ap()[c],
                    deps=[t_spv[c]] + (pass1_dead if c < 2 else [t_p2_ld[c - 2][2]]),
                    tag=f"p2v{c}")
        tc2 = P.dma("sy", s_c2[rot], d_ck.ap()[c],
                    deps=[t_ck[c]] + (pass1_dead if c < 2 else [t_p2_ld[c - 2][3]]),
                    tag=f"p2c{c}")
        t_p2_ld.append((tq2, tk2, tv2, tc2))
        if p2depth <= 1:
            return
        ts2 = TT("ve", st_bf, s_c2[rot], stG_bf[:], A.add,
                 [tc2, t_gmask] + ([prev_num_mm[0]] if prev_num_mm[0] else []),
                 f"p2st{c}")
        if p2depth <= 2:
            return
        ydeps = []
        for hh in range(2):
            h0 = hh * (H // 2)
            amdeps = [prev_am[0]] if prev_am[0] else []
            am_toks = []
            NHH = H // 2
            for h in range(h0, h0 + NHH):
                b = (h - h0) % 2
                bank = ps[b]
                col = ((h - h0) // 2) * 128
                bp = (h % 2) * 64
                lhs = s_k2[rot][bp:bp + 64, (h // 2) * 128:(h // 2) * 128 + 128]
                rhs = s_q2[rot][bp:bp + 64, (h // 2) * 128:(h // 2) * 128 + 128]
                tk3 = MM(bank[:, col:col + 128], lhs, rhs, True, True,
                         ([tq2, tk2] + amdeps) if h == h0 else [],
                         f"ab{c}.{h}", inc=1,
                         tile=((bp, 0) if bp else None))
                am_toks.append(tk3)
            ev_am = []
            for b in range(2):
                n_in = NHH // 2
                mb = maskT.ap()[:, 0:n_in * 128].rearrange("p (h t) -> p h t", t=128)
                dep_ev = [am_toks[-1], am_toks[-2], tld_mask]
                if prev_num_mm[0]:
                    dep_ev.append(prev_num_mm[0])
                s_am_v = s_am.rearrange("p (i two t) -> p two i t", two=2, t=128)[:, b]
                tev = TT("ve", s_am_v,
                         ps[b][:, 0:n_in * 128].rearrange("p (h t) -> p h t", t=128),
                         mb, A.mult, dep_ev, f"amev{c}.{hh}.{b}")
                ev_am.append(tev)
            prev_am[0] = ev_am[-1]
            if p2depth <= 3:
                continue
            num_toks = []
            for h in range(h0, h0 + NHH):
                b = (h - h0) % 2
                bank = ps[2 + b]
                col = ((h - h0) // 2) * VW
                bp = (h % 2) * 64
                lhsq = s_q2[rot][bp:bp + 64, (h // 2) * 128:(h // 2) * 128 + 128]
                rhs_st = st_bf[bp:bp + 64, (h // 2) * VW:(h // 2) * VW + VW]
                d0 = [ts2, ev_am[-1]]
                if prev_numev[0]:
                    d0.append(prev_numev[0])
                MM(bank[:, col:col + VW], lhsq, rhs_st, True, False,
                   d0 if h == h0 else [], f"nump{c}.{h}",
                   tile=((bp, 0) if bp else None))
                lhsa = s_am[:, (h - h0) * 128:(h - h0 + 1) * 128]
                rhsv = s_v2[rot][:, h * VW:(h + 1) * VW]
                t2b = MM(bank[:, col:col + VW], lhsa, rhsv, False, True,
                         [tv2] if h == h0 else [], f"numi{c}.{h}", inc=1)
                num_toks.append(t2b)
            prev_num_mm[0] = num_toks[-1]
            for b in range(2):
                n_in = NHH // 2
                bank = ps[2 + b]
                b3 = bank[:, 0:n_in * VW].rearrange("p (h w) -> p h w", w=VW)
                rc_v = s_rc.ap().rearrange("p (hh i two) -> p hh two i", hh=2,
                                           two=2)[:, hh, b]
                td1 = TS("ve", rc_v,
                         b3[:, :, DH:DH + 1].rearrange("p h x -> p (h x)"),
                         EPS_DEN, None, A.max, None,
                         [num_toks[-1], num_toks[-2]], f"den{c}.{hh}.{b}")
                td2 = RECIP(rc_v, rc_v, [td1], f"rc{c}.{hh}.{b}")
                rcb = rc_v.rearrange("p (h o) -> p h o", o=1).broadcast_to([128, n_in, DH])
                yf3 = s_yf.ap()[:, 0:n_in * DH].rearrange("p (h d) -> p h d", d=DH)
                td3 = TT("ve", yf3, b3[:, :, 0:DH], rcb, A.mult,
                         [td2] + ([prev_yf[0]] if prev_yf[0] else []),
                         f"ydiv{c}.{hh}.{b}")
                sy_v = s_y.rearrange("p (hh i two d) -> p hh two i d", hh=2,
                                     two=2, d=DH)[:, hh, b]
                td4 = CP("ve", sy_v, s_yf[:, 0:n_in * DH], [td3],
                         f"ycast{c}.{hh}.{b}")
                prev_yf[0] = td4
                prev_numev[0] = td4
                ydeps.append(td4)
        if p2depth <= 4:
            return
        gy = transpose_group([s_y[:, ic * 128:(ic + 1) * 128] for ic in range(KD)],
                             s_yT, ydeps, f"yT{c}",
                             evac_deps=([t_y[c - 1]] if c >= 1 else []))
        wo_toks = []
        for ic in range(KD):
            lhs = s_yT[:, ic * 128:(ic + 1) * 128]
            for dd in range(ND):
                d0 = []
                if ic == 0 and dd == 0:
                    d0 = [gy, t_woT_ld]
                    if prev_wo[0]:
                        d0.append(prev_wo[0])
                rhs = s_wno[:, ic * D + dd * DW:ic * D + dd * DW + DW]
                tw = MM(ps[4 + dd][:, 0:DW], lhs, rhs, ic == 0, ic == KD - 1,
                        d0, f"wo{c}.{ic}.{dd}", inc=(1 if ic == KD - 1 else 0))
                if ic == KD - 1:
                    wo_toks.append(tw)
        t_y.append(gy)
        if p2depth <= 5:
            t_y.append(gy)
            return
        txl = P.dma("sc", xst[c % 2], x_in.ap()[c * C:(c + 1) * C, :],
                    deps=([t_out[c - 2]] if c >= 2
                          else [t_ln[-1], t_xnT[-1], t_xq_done]),
                    tag=f"oxl{c}")
        ev_toks = []
        for dd in range(ND):
            tev = TT("ve", xst[c % 2][:, dd * DW:(dd + 1) * DW],
                     xst[c % 2][:, dd * DW:(dd + 1) * DW], ps[4 + dd][:, 0:DW],
                     A.add, [txl, wo_toks[dd]], f"oadd{c}.{dd}")
            ev_toks.append(tev)
        prev_wo[0] = ev_toks[-1]
        t_out.append(P.dma("sc", d_attn.ap()[c * C:(c + 1) * C, :], xst[c % 2],
                           deps=ev_toks, tag=f"out{c}"))

    gate_P_toks = []
    t_gmxs = []

    def emit_gate_chunk(j):
        rot = j % 2
        deps = [spill_toks[(KI + j) * NT + tt] for tt in range(NT)]
        deps += (pass1_dead if j < 2 else [gate_P_toks[j - 2], t_gmxs[j - 2]])
        tl = P.dma("sy", Gw[rot], d_gt.ap()[j], deps=deps, tag=f"g.ld{j}")
        sA, sB = stq[rot]
        d0 = [tl, t_invg, t_bsg, t_mg]
        if j >= 2:
            d0.append(gate_P_toks[j - 2])
        t2 = ACT(sA, Gw[rot], AF.Identity, scl[:, S_MG:S_MG + 1],
                 scl[:, S_INVG:S_INVG + 1], d0, f"g.z{j}")
        t3 = ACT(sB, sA, AF.Sigmoid, scl[:, S_BSG:S_BSG + 1],
                 scl[:, S_SG:S_SG + 1], [t2], f"g.sig{j}")
        t5 = ACT(sB, sB, AF.Identity, scl[:, S_MG:S_MG + 1], 65536.0,
                 [t3], f"g.z2{j}")
        t6 = TS("ve", sB, sB, INV_U16, -192.0, A.mult, A.add, [t5], f"g.zb{j}")
        t6b = TS("ve", sB, sB, 0.9999847412109375, None, A.min, None, [t6],
                 f"g.zc{j}")
        t7 = STT("ve", Gw[rot], sA, -MAGIC, sB, A.add, A.mult, [t6b], f"g.P{j}")
        t8 = RED("ve", pPO[:, j:j + 1], Gw[rot], A.max, [t7], f"g.mx{j}", absv=True)
        t_gmxs.append(t8)
        gate_P_toks.append(P.dma("sy", d_P.ap()[j], Gw[rot], deps=[t7],
                                 tag=f"g.sp{j}"))

    def emit_round_C():
        t1 = RED("ve", redp[:, 3:4], pPO[:, 0:KI], A.max, t_gmxs, "rC.c")
        t4 = xpart_max(redp[:, 3:4], 1, red1[0:1, 3:4], ps[0],
                       [t1, prev_am[0]], "rC")
        t5 = P.dma("gp", d_ar["C"][0].ap()[0:1, 0:1], red1[0:1, 3:4], deps=[t4],
                   tag="arC.w")
        t6 = CC("AllReduce", A.max, [list(range(NCORES))],
                [d_ar["C"][0].ap()[0:1, 0:1]], [d_ar["C"][1].ap()[0:1, 0:1]],
                [t5], "arC")
        t6r = P.dma("gp", row2[0:1, 0:1], d_ar["C"][1].ap()[0:1, 0:1],
                    deps=[t6], tag="arC.r")
        t7 = bcast_mm(row2[0:1, 0:1], 1, gmx[:, 3:4], ps[0], [t6r, t4], "arC.b")
        prev_am[0] = t7
        u1 = TS("ve", scl[:, S_T0:S_T0 + 1], gmx[:, 3:4], scl[:, S_SG:S_SG + 1],
                None, A.mult, None, [t7, t_sg, t_invg], "s3.a")
        u2, _ = scale_from_max(scl[:, S_T0:S_T0 + 1], scl[:, S_S3:S_S3 + 1],
                               None, QMAX, [u1], "s3.b")
        u3 = RECIP(scl[:, S_C3:S_C3 + 1], scl[:, S_S3:S_S3 + 1], [u2], "s3.r")
        u4 = TS("ve", scl[:, S_C3:S_C3 + 1], scl[:, S_C3:S_C3 + 1],
                scl[:, S_SG:S_SG + 1], None, A.mult, None, [u3], "s3.c")
        return u4, u2

    o_toks = []
    t_omxs = []

    def emit_o_chunk(j, tc3, tn3):
        rot = j % 2
        deps = [gate_P_toks[j]]
        if j >= 2:
            deps += [o_toks[j - 2], t_omxs[j - 2]]
        else:
            deps += cover(gate_P_toks) + [t_gmxs[-1], t_gmxs[-2]]
        tl = P.dma("sy", Gw[rot], d_P.ap()[j], deps=deps, tag=f"o.Pld{j}")
        tu = P.dma("sy", s_upr[rot], d_up.ap()[j],
                   deps=[spill_toks[j * NT + tt] for tt in range(NT)] +
                   ([o_toks[j - 2]] if j >= 2 else pass1_dead), tag=f"o.uld{j}")
        sA, sB = stq[rot]
        d0 = [tl, tc3, tn3, t_mg]
        if j >= 2:
            d0.append(o_toks[j - 2])
        t2 = ACT(sA, Gw[rot], AF.Identity, scl[:, S_MG:S_MG + 1],
                 scl[:, S_C3:S_C3 + 1], d0, f"o.z3{j}")
        t4 = ACT(sB, s_upr[rot], AF.Identity, scl[:, S_MG:S_MG + 1],
                 scl[:, S_INVU:S_INVU + 1], [tu, t_invu, t_mg], f"o.z4{j}")
        t4b = TS("ve", sB, sB, -MAGIC, None, A.add, None, [t4], f"o.z4b{j}")
        t5 = STT("ve", Gw[rot], sA, -MAGIC, sB, A.add, A.mult, [t2, t4b],
                 f"o.P2{j}")
        t6 = RED("ve", pPO[:, KI + j:KI + j + 1], Gw[rot], A.max, [t5],
                 f"o.mx{j}", absv=True)
        t_omxs.append(t6)
        o_toks.append(P.dma("sy", d_og.ap()[j], Gw[rot], deps=[t5],
                            tag=f"o.sp{j}"))

    def emit_round_D():
        t1 = RED("ve", redp[:, 0:1], pPO[:, KI:2 * KI], A.max, t_omxs, "rD.c")
        t4 = xpart_max(redp[:, 0:1], 1, red1[0:1, 0:1], ps[0],
                       [t1, prev_am[0]], "rD")
        t5 = P.dma("gp", d_ar["D"][0].ap()[0:1, 0:1], red1[0:1, 0:1], deps=[t4],
                   tag="arD.w")
        t6 = CC("AllReduce", A.max, [list(range(NCORES))],
                [d_ar["D"][0].ap()[0:1, 0:1]], [d_ar["D"][1].ap()[0:1, 0:1]],
                [t5], "arD")
        t6r = P.dma("gp", row2[0:1, 0:1], d_ar["D"][1].ap()[0:1, 0:1],
                    deps=[t6], tag="arD.r")
        t7 = bcast_mm(row2[0:1, 0:1], 1, gmx[:, 4:5], ps[0], [t6r, t4], "arD.b")
        prev_am[0] = t7
        u1 = TS("ve", scl[:, S_T1:S_T1 + 1], scl[:, S_S3:S_S3 + 1],
                scl[:, S_SU:S_SU + 1], None, A.mult, None, [t7], "so.a")
        u2 = TS("ve", scl[:, S_T0:S_T0 + 1], gmx[:, 4:5], scl[:, S_T1:S_T1 + 1],
                None, A.mult, None, [u1], "so.b")
        u3, _ = scale_from_max(scl[:, S_T0:S_T0 + 1], scl[:, S_SO:S_SO + 1],
                               None, QMAX, [u2], "so.c")
        u4 = RECIP(scl[:, S_C5:S_C5 + 1], scl[:, S_SO:S_SO + 1], [u3], "so.r")
        u5 = TS("ve", scl[:, S_C5:S_C5 + 1], scl[:, S_C5:S_C5 + 1],
                scl[:, S_T1:S_T1 + 1], None, A.mult, None, [u4], "so.d")
        return u5, u3

    oq_toks = []

    def emit_oq_chunk(j, tc5, tn5):
        rot = j % 2
        deps = [o_toks[j]]
        if j >= 2:
            deps.append(oq_toks[j - 2])
        else:
            deps += cover(o_toks) + [t_omxs[-1], t_omxs[-2]]
        tl = P.dma("sy", Gw[rot], d_og.ap()[j], deps=deps, tag=f"q.ld{j}")
        sA = stq[rot][0]
        t1 = ACT(sA, Gw[rot], AF.Identity, scl[:, S_MG:S_MG + 1],
                 scl[:, S_C5:S_C5 + 1],
                 [tl, tc5, tn5, t_mg] + ([oq_toks[j - 2]] if j >= 2 else []),
                 f"q.y{j}")
        t2a = TS("ve", sA, sA, -MAGIC, None, A.add, None, [t1], f"q.um{j}")
        t2b = TS("ve", sA, sA, 32767.0, None, A.min, None, [t2a], f"q.cl{j}")
        t3 = ACT(Gw[rot], sA, AF.Identity, 0.0, scl[:, S_SO:S_SO + 1],
                 [t2b], f"q.oq{j}")
        oq_toks.append(P.dma("sy", d_oq.ap()[j], Gw[rot], deps=[t3],
                             tag=f"q.sp{j}"))

    t_dnld, t_dnev, t_dnw, t_dnmm = [], [], [], []
    dbg_hold = []

    def emit_down():
        first_extra = [prev_am[0], prev_numev[0], prev_wo[0]]
        for c in range(NCH):
            buf = s_oq[0] if c % 2 == 0 else s_oq2
            deps = cover(oq_toks) + [t_wdT_groups[-1]] + (first_extra if c < 2 else [])
            if c >= 2:
                deps.append(t_dnmm[c - 2])
            tl = P.dma("sy",
                       buf.rearrange("p (k t) -> p k t", t=128),
                       d_oq.ap()[:, :, c * 128:(c + 1) * 128].rearrange(
                           "k p t -> p k t"), deps=deps,
                       tag=f"dn.ld{c}")
            t_dnld.append(tl)
            mm_toks = []
            for dd in range(ND):
                bank = ps[(c % 2) * 2 + dd]
                last = None
                for ici in range(KI):
                    lhs = buf[:, ici * 128:(ici + 1) * 128]
                    d0 = []
                    if ici == 0 and dd == 0:
                        d0 = [tl] + (first_extra if c < 2 else [])
                        if c >= 2:
                            d0 += [t_dnev[(c - 2) * ND + d2] for d2 in range(ND)]
                    rhs = wdT[:, ici * D + dd * DW:ici * D + dd * DW + DW]
                    last = MM(bank[:, 0:DW], lhs, rhs, ici == 0, ici == KI - 1,
                              d0, f"dn.mm{c}.{dd}.{ici}",
                              inc=(1 if ici == KI - 1 else 0))
                mm_toks.append(last)
            t_dnmm.append(mm_toks[-1])
            if dbg and c == 2:
                tdb1 = CP("ve", s_yf[:, 0:512], ps[(c % 2) * 2 + (ND - 1)][:, 0:DW],
                          [mm_toks[ND - 1]], "dbg.psum")
                dbg_hold.append(P.dma("gp", dbgd.ap()[0, :, 0:DW], s_yf[:, 0:512],
                                      deps=[tdb1], tag="dbg.psw"))
                dbg_hold.append(tdb1)
            if dbg and c == 2 and len(dbg_hold) >= 0:
                pass
            tpl = P.dma("sc", stq[c % 2][0][:, 0:D], d_attn.ap()[c * C:(c + 1) * C, :],
                        deps=[t_out[c]] + (cover(oq_toks) if c < 2 else [t_dnw[(c - 2) * ND]]),
                        tag=f"dn.pl{c}")
            for dd in range(ND):
                dstf = stq[c % 2][1][:, dd * DW:(dd + 1) * DW]
                d0 = [mm_toks[dd], tpl] + cover(oq_toks)
                if c >= 2:
                    d0.append(t_dnw[(c - 2) * ND + dd])
                te1 = CP("sc", dstf, ps[(c % 2) * 2 + dd][:, 0:DW], d0,
                         f"dn.cp{c}.{dd}")
                tev = TT("ve", dstf, dstf,
                         stq[c % 2][0][:, dd * DW:(dd + 1) * DW], A.add,
                         [te1, tpl], f"dn.ev{c}.{dd}")
                t_dnev.append(tev)
                t_dnw.append(P.dma(
                    "gp", out_hbm.ap()[c * C:(c + 1) * C, dd * DW:(dd + 1) * DW],
                    dstf, deps=[tev], tag=f"dn.w{c}.{dd}"))

    gi = 0
    wi = 0
    nwd = KD
    do_g = stage not in (72, 73, 74)
    do_w = stage not in (71, 73, 74)
    do_p = stage not in (71, 72)
    np2 = 2 if stage == 73 else NCH
    for c in range(NCH):
        upto = ((c + 1) * KI) // NCH
        while gi < upto:
            if do_g:
                emit_gate_chunk(gi)
            gi += 1
        upto_w = ((c + 1) * nwd) // NCH
        while wi < upto_w:
            if do_w:
                emit_wdT(wi)
            wi += 1
        if do_p and c < np2:
            emit_pass2_chunk(c)
    while gi < KI:
        if do_g:
            emit_gate_chunk(gi)
        gi += 1
    while wi < nwd:
        if do_w:
            emit_wdT(wi)
        wi += 1
    if stage in (71, 72, 73, 74):
        return emit_tail()
    if stage <= 8:
        return emit_tail()
    tc3, tn3 = emit_round_C()
    for j in range(KI):
        emit_o_chunk(j, tc3, tn3)
    if stage <= 9:
        return emit_tail()
    tc5, tn5 = emit_round_D()
    for j in range(KI):
        emit_oq_chunk(j, tc5, tn5)
    emit_down()

    if dbg:
        dd = (cover(t_out) + cover(t_dnw) + cover(oq_toks) + cover(o_toks) +
              cover(gate_P_toks) + cover(spill_toks) + [prev_am[0], prev_numev[0]])
        P.dma("gp", dbgb.ap()[0], d_up.ap()[0], deps=dd, tag="dbg.up")
        P.dma("gp", dbgb.ap()[1], d_gt.ap()[0], deps=dd, tag="dbg.gt")
        P.dma("gp", dbgb.ap()[2], d_P.ap()[0], deps=dd, tag="dbg.P")
        P.dma("gp", dbgb.ap()[3], d_og.ap()[0], deps=dd, tag="dbg.og")
        P.dma("gp", dbgb.ap()[4], d_oq.ap()[0], deps=dd, tag="dbg.oq")
        P.dma("gp", dbgb.ap()[5, :, 0:KD * 128], d_qT.ap()[0], deps=dd, tag="dbg.qT")
        P.dma("gp", dbgb.ap()[6, :, 0:KD * 128], d_kT.ap()[0], deps=dd, tag="dbg.kT")
        P.dma("gp", dbgb.ap()[7, :, 0:H * VW], d_va.ap()[0], deps=dd, tag="dbg.va")
        tg1 = P.dma("gp", dbgf.ap()[0, :, 0:8], gmx[:, 0:8], deps=dd, tag="dbg.gmx")
        P.dma("gp", dbgf.ap()[0, :, 24:40], scl[:, 0:16], deps=dd, tag="dbg.scl")
        P.dma("gp", dbgf.ap()[1, :, 0:D], d_attn.ap()[2 * C:3 * C, :], deps=dd, tag="dbg.attn")
        P.dma("gp", dbgf.ap()[2, :, 0:HP * VW], stG[:], deps=dd, tag="dbg.stG")
        tg2 = P.dma("gp", dbgf.ap()[3, :, 0:HP * VW],
                    d_ck.ap()[NCH // 2], deps=dd, tag="dbg.ck")
        P.dma("gp", dbgw.ap()[0], s_oq[0], deps=dd + cover(t_dnld), tag="dbg.soq")
        P.dma("gp", dbgw.ap()[1], wdT[:, 0:KI * 128], deps=dd, tag="dbg.wdT")
        P.dma("gp", dbgd.ap()[1, :, 0:DW],
              a_A.ap()[:, T + DW:T + 2 * DW] if False else arA.t.ap()[:, T + DW:T + 2 * DW],
              deps=dd, tag="dbg.part")
        P.op("gp", lambda e: e.nop(),
             cover(t_out) + cover(t_dnw) + [tg1, tg2] +
             [(s, P.count[s]) for s in ("gpd0", "gpd1", "gpd2", "gpd3", "gpd4", "gpd5")],
             1, "fin")
    else:
        P.op("gp", lambda e: e.nop(), cover(t_out) + cover(t_dnw), 1, "fin")

    P.verify()
    blk = es.enter_context(nc.Block())

    @blk.sync
    def _(eng):
        P.emit_engine("sy", eng, sems)

    @blk.scalar
    def _(eng):
        P.emit_engine("sc", eng, sems)

    @blk.vector
    def _(eng):
        P.emit_engine("ve", eng, sems)

    @blk.gpsimd
    def _(eng):
        P.emit_engine("gp", eng, sems)

    @blk.tensor
    def _(eng):
        P.emit_engine("pe", eng, sems)

    es.close()
    return nc


def _host_tables(cfg, core):
    T, D, H = cfg["T"], cfg["D"], cfg["H"]
    DH = D // H
    NCH = T // 128
    half = core % 2
    pos0 = half * T
    import ml_dtypes
    identf = np.eye(128, dtype=np.float32)
    identb = identf.astype(ml_dtypes.bfloat16)
    HB_ = min(4, H // 2)
    mask1 = np.zeros((128, 128), dtype=np.float32)
    for p in range(128):
        mask1[p, p:] = 1.0
    mask = np.tile(mask1, (1, HB_))
    halfw = DH // 2
    inv_freq = (10000.0 ** (-np.arange(halfw, dtype=np.float64) / halfw))
    ctab = np.zeros((128, NCH * 32), dtype=np.float32)
    stab = np.zeros((128, NCH * 32), dtype=np.float32)
    for c in range(NCH):
        pos = pos0 + c * 128 + np.arange(128)
        ang = pos[:, None] * inv_freq[None, :]
        ctab[:, c * 32:(c + 1) * 32] = np.cos(ang).astype(np.float32)
        stab[:, c * 32:(c + 1) * 32] = np.sin(ang).astype(np.float32)
    flags = np.zeros((1, 8), dtype=np.float32)
    flags[0, 0] = 1.0 if half == 0 else 0.0
    flags[0, 1] = 1.0 if half == 1 else 0.0
    return identb, identf, mask, ctab, stab, flags


def _shard_inputs(cfg, inputs):
    T, D, H, I = cfg["T"], cfg["D"], cfg["H"], cfg["I"]
    KI = I // 128
    KD = D // 128
    TUG = max(1, (KI + NCORES - 1) // NCORES)
    TD = max(1, (KD + NCORES - 1) // NCORES)
    x = np.ascontiguousarray(np.asarray(inputs["L0_attn_x"], dtype=np.float32))
    B, Tfull, _ = x.shape
    maps = []
    for core in range(NCORES):
        b, half = core // 2, core % 2
        xs = np.ascontiguousarray(x[b, half * T:(half + 1) * T, :])
        identb, identf, mask, ctab, stab, flags = _host_tables(cfg, core)
        rows_ug = [((core * TUG + u) % KI) * 128 for u in range(TUG)]
        wup = np.concatenate([np.asarray(inputs["L0_ffn_w_up"], np.float32)[r:r + 128]
                              for r in rows_ug], axis=0)
        wg = np.concatenate([np.asarray(inputs["L0_ffn_w_gate"], np.float32)[r:r + 128]
                             for r in rows_ug], axis=0)
        rows_d = [((core * TD + u) % KD) * 128 for u in range(TD)]
        wd = np.concatenate([np.asarray(inputs["L0_ffn_w_down"], np.float32)[r:r + 128]
                             for r in rows_d], axis=0)
        maps.append({
            "x": xs,
            "Wq": np.ascontiguousarray(np.asarray(inputs["L0_attn_Wq"], np.float32)),
            "Wk": np.ascontiguousarray(np.asarray(inputs["L0_attn_Wk"], np.float32)),
            "Wv": np.ascontiguousarray(np.asarray(inputs["L0_attn_Wv"], np.float32)),
            "Wo": np.ascontiguousarray(np.asarray(inputs["L0_attn_Wo"], np.float32)),
            "wup_sl": np.ascontiguousarray(wup),
            "wg_sl": np.ascontiguousarray(wg),
            "wd_sl": np.ascontiguousarray(wd),
            "identb": identb,
            "identf": identf,
            "mask": mask,
            "ctab": ctab,
            "stab": stab,
            "flags": flags,
        })
    return maps


_NC_CACHE = {}


def kernel(L0_attn_x, L0_attn_Wq, L0_attn_Wk, L0_attn_Wv, L0_attn_Wo,
           L0_ffn_w_gate, L0_ffn_w_up, L0_ffn_w_down):
    cfg = FULL
    inputs = dict(L0_attn_x=L0_attn_x, L0_attn_Wq=L0_attn_Wq,
                  L0_attn_Wk=L0_attn_Wk, L0_attn_Wv=L0_attn_Wv,
                  L0_attn_Wo=L0_attn_Wo, L0_ffn_w_gate=L0_ffn_w_gate,
                  L0_ffn_w_up=L0_ffn_w_up, L0_ffn_w_down=L0_ffn_w_down)
    key = tuple(sorted(cfg.items()))
    if key not in _NC_CACHE:
        _NC_CACHE[key] = build_nc(cfg)
    nc = _NC_CACHE[key]
    in_maps = _shard_inputs(cfg, inputs)
    res = run_bass_kernel_spmd(nc, in_maps, core_ids=list(range(NCORES)))
    T, D = cfg["T"], cfg["D"]
    B = np.asarray(L0_attn_x).shape[0]
    out = np.zeros((B, 2 * T, D), dtype=np.float32)
    for core in range(NCORES):
        b, half = core // 2, core % 2
        out[b, half * T:(half + 1) * T, :] = res.results[core]["out"]
    return out
